# version 05 (final)
# baseline (speedup 1.0000x reference)
import os
os.environ.setdefault("BASS_DISABLE_FRAME_TO_TRACEBACK", "1")

import contextlib
import numpy as np
import concourse.bass as bass
import concourse.mybir as mybir
from concourse.bass_utils import run_bass_kernel_spmd
from concourse.ordered_engine_set import OrderedEngineSet

F32 = mybir.dt.float32
BF16 = mybir.dt.bfloat16
AF = mybir.ActivationFunctionType
ALU = mybir.AluOpType

D, H, ED, NS, KC, RK = 1024, 4096, 2048, 16, 15, 64
TL, TPOST, TCH, PAD, EPS = 8192, 4096, 512, 14, 1e-5
PO = TL - TPOST
NCH = TL // TCH
NPO = TPOST // TCH

_MULT = [1]


class _Sem:
    def __init__(self, h):
        self.h = h
        self.total = 0

    def inc(self, instr, n=1):
        instr.then_inc(self.h, n)
        self.total += n * _MULT[0]
        return instr

    def seed(self, eng, n):
        eng.sem_inc(self.h, n)
        self.total += n


def build(stage=0, debug_outputs=()):
    nc = bass.Bass("TRN2", target_bir_lowering=False)
    st = contextlib.ExitStack()
    _pool = [_Sem(st.enter_context(nc.semaphore(f"sp{i}"))) for i in range(18)]
    _pool_i = [0]

    def sem(name):
        s = _pool[_pool_i[0] % len(_pool)]
        _pool_i[0] += 1
        return s

    _ph = [None]

    def _pb():
        _ph[0] = contextlib.ExitStack()

    def _pe():
        _ph[0].close()
        _ph[0] = None

    def sbuf(name, shape, dt):
        stk = _ph[0] if _ph[0] is not None else st
        return stk.enter_context(nc.sbuf_tensor(name, shape, dt))

    psum = lambda name, shape: st.enter_context(
        nc.psum_tensor(name, shape, F32))

    @contextlib.contextmanager
    def kloop(eng, n):
        engines = OrderedEngineSet([eng.engine])
        name = f"kl_{nc.next_id()}"
        regs = nc.alloc_registers(f"{name}_i", engines=engines)
        nc.regs_mov(regs, 0)
        nc.br(f"{name}_loop", engines=engines)
        _MULT[0] *= n
        with nc.body(f"{name}_loop", valid_engines=engines):
            yield None
            nc.regs_alu(regs, regs, 1, op=ALU.add)
            nc.br_lt(regs, n, on_true=f"{name}_loop",
                     on_false=f"{name}_end", engines=engines)
        _MULT[0] //= n
        nc.switch_bb(f"{name}_end")
        for h in regs.handles:
            nc.free_register(h)

    class C:

        _n = [0]

        def __init__(self, eng, init):
            C._n[0] += 1
            self.eng = eng
            self.reg = eng.alloc_register(f"c{C._n[0]}")
            eng.reg_mov(self.reg, int(init))

        def add(self, step):
            if step:
                self.eng.reg_alu(self.reg, self.reg, int(step), op=ALU.add)

        def free(self):
            self.eng.free_register(self.reg)

    class Cs:

        def __init__(self, eng):
            self.eng = eng
            self.lst = []

        def new(self, init):
            c = C(self.eng, init)
            self.lst.append(c)
            return c

        def free(self):
            for c in self.lst:
                c.free()
            self.lst.clear()

    def dyn(tmpl_ap, ctr):
        return bass.AP(tmpl_ap.tensor, ctr.reg,
                       [list(p) for p in tmpl_ap.ap])

    def W(eng, semh, ctr, step):
        eng.wait_ge(semh, ctr.reg)
        ctr.add(step)

    def reflect(eng, ctr, ksum):
        eng.reg_alu(ctr.reg, ctr.reg, int(ksum), op=ALU.subtract)
        eng.reg_alu(ctr.reg, ctr.reg, -1, op=ALU.mult)

    EI = lambda n, s, d=F32: nc.dram_tensor(n, s, d, kind="ExternalInput")
    x_f = EI("x_f", [8, 128, TL])
    maskv = EI("maskv", [1, TL])
    NWALL = 2 * D * H + 2 * H * H + H * D + D * 2 * ED + ED * D
    wall = EI("wall", [NWALL])
    wxT = EI("wxT", [ED, 128])
    wdtT = EI("wdtT", [RK, ED])
    gammav, betav = EI("gammav", [128, 8]), EI("betav", [128, 8])
    bdtv = EI("bdtv", [128, 16])
    dparv = EI("dparv", [128, 16])
    alogv = EI("alogv", [128, 256])
    eyev = EI("eyev", [128, 128])
    convwbv = EI("convwbv", [128, 16, KC + 1])
    s1v, s2v = EI("s1v", [128, 32]), EI("s2v", [128, 32])
    sh1v, sh2v = EI("sh1v", [128, 32]), EI("sh2v", [128, 32])
    s3v = EI("s3v", [128, 8])
    binxv, binzv = EI("binxv", [128, 16]), EI("binzv", [128, 16])
    boutv = EI("boutv", [128, 8])

    out_f = nc.dram_tensor("out_f", [8, 128, TPOST], F32,
                           kind="ExternalOutput")

    def idram(name, shape, dt):
        kind = "ExternalOutput" if name in debug_outputs else "Internal"
        return nc.dram_tensor(name, shape, dt, kind=kind)

    wallb = idram("wallb", [NWALL], BF16)
    _woff = {}
    _o = 0
    for _nm, _r, _c in [("w1b", D, H), ("w2b", D, H), ("h1b", H, H),
                        ("h2b", H, H), ("w3b", H, D), ("winb", D, 2 * ED),
                        ("woutb", ED, D)]:
        _woff[_nm] = (_o, _r, _c)
        _o += _r * _c
    assert _o == NWALL

    def wview(nm):
        off, r, c = _woff[nm]
        return wallb.ap()[off:off + r * c].rearrange(
            "(kg p o) -> p kg o", p=128, o=c)

    xn1_d = idram("xn1_d", [8, 128, TL], BF16)
    a_d = idram("a_d", [32, 128, TL], BF16)
    bb_d = idram("bb_d", [32, 128, TL], BF16)
    cc_d = idram("cc_d", [32, 128, TL], BF16)
    r1_d = idram("r1_d", [8, 128, TL], F32)
    lnr_d = idram("lnr_d", [8, 128, TL], F32)
    xn2_d = idram("xn2_d", [8, 128, TL], BF16)
    xin_d = idram("xin_d", [16, 128, PAD + TL], BF16)
    z_d = idram("z_d", [16, 128, TL], F32)
    xc_d = idram("xc_d", [16, 128, TL], F32)
    dmxc_d = idram("dmxc_d", [NCH, 2, 16, 128, TCH], F32)
    xcb_d = idram("xcb_d", [16, 128, TL], BF16)
    bc_d = idram("bc_d", [2 * NS, TL], F32)
    y_d = idram("y_d", [NPO, 16, 128, TCH], F32)
    yg_d = idram("yg_d", [16, 128, TPOST], BF16)
    r2_d = idram("r2_d", [8, 128, TPOST], F32)
    xn3_d = idram("xn3_d", [8, 128, TPOST], BF16)
    a2_d = idram("a2_d", [32, 128, TPOST], BF16)
    bb2_d = idram("bb2_d", [32, 128, TPOST], BF16)
    cc2_d = idram("cc2_d", [32, 128, TPOST], BF16)

    gam_sb = sbuf("gam_sb", [128, 8], F32)
    bet_sb = sbuf("bet_sb", [128, 8], F32)
    bdt_sb = sbuf("bdt_sb", [128, 16], F32)
    dpar_sb = sbuf("dpar_sb", [128, 16], F32)
    A_sb = sbuf("A_sb", [128, 256], F32)
    s1_sb = sbuf("s1_sb", [128, 32], F32)
    s2_sb = sbuf("s2_sb", [128, 32], F32)
    sh1_sb = sbuf("sh1_sb", [128, 32], F32)
    sh2_sb = sbuf("sh2_sb", [128, 32], F32)
    binx_sb = sbuf("binx_sb", [128, 16], F32)
    binz_sb = sbuf("binz_sb", [128, 16], F32)
    bout_sb = sbuf("bout_sb", [128, 8], F32)
    ones_c = sbuf("ones_c", [128, 1], F32)
    ones_r = sbuf("ones_r", [1, 128], F32)
    eye_sb = sbuf("eye_sb", [128, 128], F32)
    carry = sbuf("carry", [128, 256], F32)
    s3h_sb = sbuf("s3h_sb", [128, 8], F32)

    PS = [psum(f"PS{i}", [128, 512]) for i in range(8)]

    L0 = sem("L0")
    with nc.Block() as blk:
        @blk.gpsimd
        def _(g):
            for dst, src in [(gam_sb, gammav), (bet_sb, betav),
                             (bdt_sb, bdtv), (dpar_sb, dparv),
                             (A_sb, alogv), (eye_sb, eyev),
                             (s1_sb, s1v), (s2_sb, s2v),
                             (sh1_sb, sh1v), (sh2_sb, sh2v),
                             (binx_sb, binxv), (binz_sb, binzv),
                             (bout_sb, boutv)]:
                L0.inc(g.dma_start(out=dst[:], in_=src.ap()), 16)
            L0.inc(g.dma_start(out=s3h_sb[:], in_=s3v.ap()), 16)

        @blk.vector
        def _(v):
            v.memset(ones_c[:], 1.0)
            v.memset(ones_r[:], 1.0)
            v.memset(carry[:], 0.0)
            v.wait_ge(L0.h, L0.total)
            v.tensor_scalar(s3h_sb[:], s3h_sb[:], 0.5, None, ALU.mult)

        @blk.scalar
        def _(sc):
            sc.wait_ge(L0.h, L0.total)
            sc.activation(A_sb[:], A_sb[:], AF.Exp)

    with nc.Block() as blk:
        @blk.vector
        def _(v):
            v.tensor_scalar(A_sb[:], A_sb[:], -1.0, None, ALU.mult)


    WCa, WSa = sem("WCa"), sem("WSa")
    _pb()
    ct2 = sbuf("ct2", [128, 2, 4096], BF16)
    NCST = NWALL // (128 * 4096)
    wsv = wall.ap().rearrange("(c p t) -> c p t", p=128, t=4096)
    wdv = wallb.ap().rearrange("(c p t) -> c p t", p=128, t=4096)
    ctv = [ct2.ap()[:, e:e + 1].squeeze(1) for e in (0, 1)]
    with nc.Block() as blk:
        @blk.gpsimd
        def _(g):
            WSa.seed(g, 32)
            cs = Cs(g)
            wst = cs.new(16)
            off = cs.new(0)
            with kloop(g, NCST // 2):
                for e in range(2):
                    W(g, WSa.h, wst, 16)
                    WCa.inc(g.dma_start(
                        out=ctv[e],
                        in_=dyn(wsv[0:1].squeeze(0), off)), 16)
                    off.add(128 * 4096)
            cs.free()

        @blk.sync
        def _(sy):
            cs = Cs(sy)
            wct = cs.new(16)
            off = cs.new(0)
            with kloop(sy, NCST // 2):
                for e in range(2):
                    W(sy, WCa.h, wct, 16)
                    WSa.inc(sy.dma_start(
                        out=dyn(wdv[0:1].squeeze(0), off),
                        in_=ctv[e]), 16)
                    off.add(128 * 4096)
            cs.free()
            sy.wait_ge(WSa.h, WSa.total)

    with nc.Block() as blk:
        @blk.vector
        def _(v):
            v.memset(ct2[:, 0, :], 0.0)

    ZP = sem("ZP")
    with nc.Block() as blk:
        @blk.sync
        def _(sy):
            ZP.inc(sy.dma_start(
                out=xin_d.ap()[:, :, 0:PAD].transpose([1, 0, 2]),
                in_=ct2.ap()[:, 0, 0:16 * PAD].rearrange(
                    "p (g c) -> p g c", g=16)), 16)
            sy.wait_ge(ZP.h, ZP.total)
    _pe()

    def P_ln(tag, src_d, xn_out, T, double, sum_out=None):
        _pb()
        nch = T // TCH
        R = 2 if double else 1
        apc = R + (1 if double else 0)
        hs = sum_out is not None
        LD, SQ, SM, EV1, DV1, EV2, MR, DV2, BC, AP_, ST_ = (
            sem(f"{tag}{x}") for x in
            ("LD", "SQ", "SM", "E1", "D1", "E2", "MR", "D2", "BC", "AP",
             "ST"))
        b = dict(LD=LD.total, SQ=SQ.total, SM=SM.total, E1=EV1.total,
                 D1=DV1.total, E2=EV2.total, MR=MR.total, D2=DV2.total,
                 BC=BC.total, AP=AP_.total, ST=ST_.total)
        xt1 = sbuf(f"{tag}x1", [128, 8, TCH], F32)
        xt = [xt1, xt1]
        sqt = sbuf(f"{tag}sq", [128, 8, TCH], F32)
        u1t = sbuf(f"{tag}u1", [128, 8, TCH], F32)
        nt = sbuf(f"{tag}nt", [128, 8, TCH], F32)
        xnt1 = sbuf(f"{tag}xn1s", [128, 8, TCH], BF16)
        xnt = [xnt1, xnt1]
        rma = sbuf(f"{tag}ra", [1, TCH], F32)
        rmb = sbuf(f"{tag}rb", [1, TCH], F32)
        rmm = sbuf(f"{tag}rm", [1, TCH], F32)
        rmc = sbuf(f"{tag}rc", [1, TCH], F32)
        rmr = sbuf(f"{tag}rr", [1, TCH], F32)
        srcv = src_d.ap().rearrange("g p t -> p g t")
        dstv = xn_out.ap().rearrange("g p t -> p g t")
        if hs:
            SR = sem(f"{tag}SR")
            b2 = dict(SR=SR.total)
            sumv = sum_out.ap().rearrange("g p t -> p g t")

        with nc.Block() as blk:
            @blk.sync
            def _(sy):
                ST_.seed(sy, 16)
                cs = Cs(sy)
                if double:
                    apf = cs.new(b["AP"] + apc)
                    apr = cs.new(b["AP"] + 2 * apc)
                else:
                    apf = cs.new(b["D2"] + 0)
                    apr = cs.new(b["D2"] + 1)
                aph = AP_.h if double else DV2.h
                stp = apc if double else 1
                aso = cs.new(int(srcv[:, :, 0:TCH].offset))
                ado = cs.new(int(dstv[:, :, 0:TCH].offset))
                so_of = cs.new(int(sumv[:, :, 0:TCH].offset)) if hs else None
                with kloop(sy, nch):
                    W(sy, aph, apf, stp)
                    LD.inc(sy.dma_start(
                        out=xt1[:],
                        in_=dyn(srcv[:, :, 0:TCH], aso)), 16)
                    aso.add(TCH)
                    W(sy, aph, apr, stp)
                    ST_.inc(sy.dma_start(
                        out=dyn(dstv[:, :, 0:TCH], ado),
                        in_=xnt1[:]), 16)
                    ado.add(TCH)
                    if hs:
                        SR.inc(sy.dma_start(
                            out=dyn(sumv[:, :, 0:TCH], so_of),
                            in_=sqt.ap()), 16)
                        so_of.add(TCH)
                cs.free()
                sy.wait_ge(ST_.h, ST_.total)
                if hs:
                    sy.wait_ge(SR.h, SR.total)

            @blk.scalar
            def _(sc):
                if double:
                    AP_.seed(sc, apc)
                cs = Cs(sc)
                ld = cs.new(b["LD"] + 16)
                dv2r = cs.new(b["D2"] + 1) if double else None
                sm = cs.new(b["SM"] + 1)
                mr = cs.new(b["MR"] + 0)
                dv1 = cs.new(b["D1"] + 1)
                dv2g = cs.new(b["D2"] + 2) if double else None
                stg = cs.new(b["ST"] + 32 + 16 - 32 + 16) if double else None
                if double:
                    sc.reg_mov(stg.reg, b["ST"] + 16)
                ap3 = cs.new(b["AP"] + apc) if hs else None
                sr2 = cs.new(b2["SR"] + 0) if hs else None
                with kloop(sc, nch):
                    for e in (0,):
                        W(sc, LD.h, ld, 16)
                        if hs:
                            W(sc, AP_.h, ap3, apc)
                            W(sc, SR.h, sr2, 16)
                        for r in range(R):
                            src_t = xt[e] if r == 0 else u1t
                            if r == 1:
                                W(sc, DV2.h, dv2r, R)
                            last = None
                            for dg in range(8):
                                last = sc.activation(sqt[:, dg, :],
                                                     src_t[:, dg, :],
                                                     AF.Square)
                            SQ.inc(last, 1)
                            W(sc, SM.h, sm, 1)
                            W(sc, MR.h, mr, 1)
                            sc.activation(rmm[:], PS[0][0:1, :], AF.Copy,
                                          scale=np.float32(1.0 / D))
                            sc.activation(rma[:], PS[1][0:1, :], AF.Copy,
                                          scale=np.float32(1.0 / D))
                            EV1.inc(sc.activation(rmb[:], rmm[:],
                                                  AF.Square), 1)
                            W(sc, DV1.h, dv1, 1)
                            sc.activation(rma[:], rmb[:], AF.Ln,
                                          bias=np.float32(EPS))
                            EV2.inc(sc.activation(rmr[:], rma[:], AF.Exp,
                                                  scale=np.float32(-0.5)), 1)
                        if double:
                            W(sc, DV2.h, dv2g, 2)
                            W(sc, ST_.h, stg, 16)
                            last = None
                            for dg in range(8):
                                last = sc.activation(
                                    xnt[e][:, dg, :], nt[:, dg, :],
                                    AF.Identity,
                                    scale=gam_sb[:, dg:dg + 1],
                                    bias=bet_sb[:, dg:dg + 1])
                            AP_.inc(last, 1)
                cs.free()

            @blk.vector
            def _(v):
                if not double:
                    AP2seed = 0
                cs = Cs(v)
                stv = cs.new(b["ST"] + 16) if not double else None
                ev1 = cs.new(b["E1"] + 1)
                ev2 = cs.new(b["E2"] + 1)
                bcm = cs.new(b["BC"] + 0)
                bc = cs.new(b["BC"] + 1)
                sr = cs.new(b2["SR"] + 0) if hs else None
                with kloop(v, nch):
                    for e in (0,):
                        if not double:
                            W(v, ST_.h, stv, 16)
                        for r in range(R):
                            src_t = xt[e] if r == 0 else u1t
                            W(v, EV1.h, ev1, 1)
                            DV1.inc(v.tensor_sub(rmb[:], rma[:], rmb[:]), 1)
                            W(v, EV2.h, ev2, 1)
                            W(v, BC.h, bcm, 1)
                            MR.inc(v.tensor_mul(rmc[:], rmm[:], rmr[:]), 1)
                            W(v, BC.h, bc, 1)
                            dst_t = (u1t if r == 0 else nt) if double \
                                else None
                            last = None
                            lmul = None
                            for dg in range(8):
                                lmul = v.tensor_mul(sqt[:, dg, :],
                                                    src_t[:, dg, :],
                                                    PS[2][:, 0:TCH])
                                tgt = (dst_t[:, dg, :] if dst_t is not None
                                       else xnt[e][:, dg, :])
                                last = v.tensor_sub(tgt, sqt[:, dg, :],
                                                    PS[3][:, 0:TCH])
                            if double and r == 0:
                                AP_.inc(lmul, 1)
                            DV2.inc(last, 1)
                        if hs:
                            W(v, SR.h, sr, 16)
                            last = None
                            for dg in range(8):
                                last = v.tensor_add(sqt[:, dg, :],
                                                    xt[e][:, dg, :],
                                                    u1t[:, dg, :])
                            AP_.inc(last, 1)
                cs.free()

            @blk.tensor
            def _(t):
                cs = Cs(t)
                sq = cs.new(b["SQ"] + 1)
                ev1f = cs.new(b["E1"] + 0)
                dv2f = cs.new(b["D2"] + 0)
                mrt = cs.new(b["MR"] + 1)
                with kloop(t, nch):
                    for e in (0,):
                        for r in range(R):
                            src_t = xt[e] if r == 0 else u1t
                            W(t, SQ.h, sq, 1)
                            W(t, EV1.h, ev1f, 1)
                            for dg in range(8):
                                t.matmul(PS[0][0:1, :], ones_c[:],
                                         src_t[:, dg, :],
                                         start=(dg == 0), stop=(dg == 7))
                            last = None
                            for dg in range(8):
                                last = t.matmul(PS[1][0:1, :], ones_c[:],
                                                sqt[:, dg, :],
                                                start=(dg == 0),
                                                stop=(dg == 7))
                            SM.inc(last, 1)
                            W(t, DV2.h, dv2f, 1)
                            W(t, MR.h, mrt, 1)
                            t.matmul(PS[2][:, 0:TCH], ones_r[:], rmr[:],
                                     start=True, stop=True)
                            BC.inc(t.matmul(PS[3][:, 0:TCH], ones_r[:],
                                            rmc[:], start=True, stop=True),
                                   1)
                cs.free()
        _pe()

    def P_stream(tag, inp_d, KG, wsrc, OG, out_d, T, actf,
                 scale_sb=None, bias_sb=None, out_dt=BF16, dve="none",
                 res_d=None, store_off=0, res_off=0, wcol_off=0):
        _pb()
        TBW = 1024
        NSUB = TBW // 512
        nblk = T // TBW
        LD, WL, WLB, M1, EVa, DVa, ST_ = (sem(f"{tag}{x}") for x in
                                          ("LD", "WL", "WB", "M1", "EV",
                                           "DV", "ST"))
        has_s = scale_sb is not None
        has_b = bias_sb is not None
        has_r = dve == "res"
        need_dve = dve != "none"
        b = dict(LD=LD.total, WL=WL.total, WB=WLB.total, M1=M1.total,
                 EV=EVa.total, DV=DVa.total, ST=ST_.total)
        inb = sbuf(f"{tag}in", [128, KG, TBW], BF16)
        mkb = sbuf(f"{tag}mk", [128, TBW], F32) if dve == "mask" else None
        resb = sbuf(f"{tag}res", [128, 8, TBW], F32) if has_r else None
        wt = [sbuf(f"{tag}w{i}", [128, KG, 128], BF16) for i in range(2)]
        ft = sbuf(f"{tag}ft", [128, TBW], F32) if need_dve else None
        ob = sbuf(f"{tag}ob", [128, OG, TBW], out_dt)
        inv = inp_d.ap().rearrange("g p t -> p g t")
        outv = out_d.ap().rearrange("g p t -> p g t")
        resv = res_d.ap().rearrange("g p t -> p g t") if has_r else None
        wv = wsrc if isinstance(wsrc, bass.AP) else \
            wsrc.ap().rearrange("(kg p) o -> p kg o", p=128)
        if wcol_off:
            wv = wv[:, :, wcol_off:]
        nld = 16 * (1 + (dve == "mask") + has_r)

        with nc.Block() as blk:
            @blk.gpsimd
            def _(g):
                cs = Cs(g)
                ldg = cs.new(b["M1"] + 0)
                dvg = cs.new(b["DV"] + 2) if need_dve else None
                in_of = cs.new(int(inv[:, :, 0:TBW].offset))
                mk_of = cs.new(int(maskv.ap()[0:1, 0:TBW].offset)) \
                    if dve == "mask" else None
                r_of = cs.new(int(resv[:, :, res_off:res_off + TBW].offset)) \
                    if has_r else None
                wlga = cs.new(b["EV"] + NSUB)
                with kloop(g, nblk):
                    W(g, M1.h, ldg, OG)
                    LD.inc(g.dma_start(out=inb[:],
                                       in_=dyn(inv[:, :, 0:TBW], in_of)), 16)
                    in_of.add(TBW)
                    if dve == "mask":
                        W(g, DVa.h, dvg, OG)
                        LD.inc(g.dma_start(
                            out=mkb[:],
                            in_=dyn(maskv.ap()[0:1, 0:TBW], mk_of)
                            .partition_broadcast(128).squeeze(1)), 16)
                        mk_of.add(TBW)
                    if has_r:
                        W(g, DVa.h, dvg, OG)
                        LD.inc(g.dma_start(
                            out=resb[:],
                            in_=dyn(resv[:, :, 0:TBW], r_of)), 16)
                        r_of.add(TBW)
                    for o in range(0, OG, 2):
                        W(g, EVa.h, wlga, 2 * NSUB)
                        WL.inc(g.dma_start(
                            out=wt[0][:],
                            in_=wv[:, :, o * 128:(o + 1) * 128]), 16)
                cs.free()

            @blk.sync
            def _(sy):
                cs = Cs(sy)
                wlgb = cs.new(b["EV"] + 2 * NSUB)
                with kloop(sy, nblk):
                    for o in range(1, OG, 2):
                        W(sy, EVa.h, wlgb, 2 * NSUB)
                        WLB.inc(sy.dma_start(
                            out=wt[1][:],
                            in_=wv[:, :, o * 128:(o + 1) * 128]), 16)
                cs.free()

            @blk.tensor
            def _(t):
                cs = Cs(t)
                ld = cs.new(b["LD"] + nld)
                wl = cs.new(b["WL"] + 16)
                wlb2 = cs.new(b["WB"] + 16)
                evb = cs.new(b["EV"] + NSUB)
                with kloop(t, nblk):
                    W(t, LD.h, ld, nld)
                    with kloop(t, OG // 2):
                        for e in range(2):
                            W(t, (WL if e == 0 else WLB).h,
                              wl if e == 0 else wlb2, 16)
                            W(t, EVa.h, evb, NSUB)
                            last = None
                            for kh in range(KG):
                                for s_ in range(NSUB):
                                    last = t.matmul(
                                        PS[e * NSUB + s_][:],
                                        wt[e][:, kh, :],
                                        inb[:, kh,
                                            s_ * 512:(s_ + 1) * 512],
                                        start=(kh == 0),
                                        stop=(kh == KG - 1))
                            M1.inc(last, 1)
                cs.free()

            @blk.scalar
            def _(sc):
                EVa.seed(sc, 2 * NSUB)
                ST_.seed(sc, 16)
                cs = Cs(sc)
                m1 = cs.new(b["M1"] + 1)
                stf = cs.new(b["ST"] + 16)
                dvo = cs.new(b["DV"] + 2) if need_dve else None
                dvs = cs.new(b["DV"] + 2 + OG) if need_dve else None
                o_of = cs.new(int(outv[:, 0:1,
                                       store_off:store_off + TBW].offset))
                with kloop(sc, nblk):
                    W(sc, ST_.h, stf, 16)
                    for o in range(OG):
                        e = o % 2
                        W(sc, M1.h, m1, 1)
                        if need_dve:
                            W(sc, DVa.h, dvo, 1)
                        kw = {}
                        if has_s:
                            kw["scale"] = scale_sb[:, o:o + 1]
                        if has_b:
                            kw["bias"] = bias_sb[:, o:o + 1]
                        last = None
                        for s_ in range(NSUB):
                            dst = (ft[:, s_ * 512:(s_ + 1) * 512]
                                   if need_dve else
                                   ob[:, o, s_ * 512:(s_ + 1) * 512])
                            last = sc.activation(dst, PS[e * NSUB + s_][:],
                                                 actf, **kw)
                        EVa.inc(last, NSUB)
                    if need_dve:
                        W(sc, DVa.h, dvs, OG)
                    ST_.inc(sc.dma_start(
                        out=dyn(outv[:, 0:OG, 0:TBW], o_of),
                        in_=ob.ap()), 16)
                    o_of.add(TBW)
                cs.free()
                sc.wait_ge(ST_.h, ST_.total)

            if need_dve:
                @blk.vector
                def _(v):
                    DVa.seed(v, 2)
                    cs = Cs(v)
                    ev = cs.new(b["EV"] + 3 * NSUB)
                    with kloop(v, nblk):
                        for o in range(OG):
                            W(v, EVa.h, ev, NSUB)
                            if has_r:
                                DVa.inc(v.tensor_add(ob[:, o, :], ft[:],
                                                     resb[:, o % 8, :]), 1)
                            else:
                                DVa.inc(v.tensor_mul(ob[:, o, :], ft[:],
                                                     mkb[:]), 1)
                    cs.free()
        _pe()


    def P_ffn_in(tag, xnsrc, aout, w1v_, w2v_, T):
        _pb()
        TBW = 1024
        NSUB = 2
        OG = 32
        nblk = T // TBW
        LD, WL, WL2, M1, M2, EVa, EVb, GL, ST_ = (
            sem(f"{tag}{x}") for x in
            ("LD", "WL", "W2", "M1", "M2", "Ea", "Eb", "GL", "ST"))
        b = dict(LD=LD.total, WL=WL.total, W2=WL2.total, M1=M1.total,
                 M2=M2.total, Ea=EVa.total, Eb=EVb.total, GL=GL.total,
                 ST=ST_.total)
        inb = sbuf(f"{tag}in", [128, 8, TBW], BF16)
        wt = [sbuf(f"{tag}w{i}", [128, 8, 128], BF16) for i in range(4)]
        p1t = sbuf(f"{tag}p1", [128, TBW], F32)
        p2t = sbuf(f"{tag}p2", [128, TBW], F32)
        ab = sbuf(f"{tag}ab", [128, OG, TBW], BF16)
        xnv = xnsrc.ap().rearrange("g p t -> p g t")
        av = aout.ap().rearrange("g p t -> p g t")

        with nc.Block() as blk:
            @blk.gpsimd
            def _(g):
                cs = Cs(g)
                ldg = cs.new(b["M2"] + 0)
                in_of = cs.new(int(xnv[:, :, 0:TBW].offset))
                wla = cs.new(b["Ea"] + 0)
                with kloop(g, nblk):
                    W(g, M2.h, ldg, OG)
                    LD.inc(g.dma_start(out=inb[:],
                                       in_=dyn(xnv[:, :, 0:TBW], in_of)),
                           16)
                    in_of.add(TBW)
                    for o in range(OG):
                        e = o % 2
                        W(g, EVa.h, wla, NSUB)
                        WL.inc(g.dma_start(
                            out=wt[e][:],
                            in_=w1v_[:, :, o * 128:(o + 1) * 128]), 16)
                cs.free()

            @blk.sync
            def _(sy):
                cs = Cs(sy)
                wlb = cs.new(b["Eb"] + 0)
                with kloop(sy, nblk):
                    for o in range(OG):
                        e = o % 2
                        W(sy, EVb.h, wlb, NSUB)
                        WL2.inc(sy.dma_start(
                            out=wt[2 + e][:],
                            in_=w2v_[:, :, o * 128:(o + 1) * 128]), 16)
                cs.free()

            @blk.tensor
            def _(t):
                cs = Cs(t)
                ld = cs.new(b["LD"] + 16)
                wl = cs.new(b["WL"] + 16)
                wl2 = cs.new(b["W2"] + 16)
                ea = cs.new(b["Ea"] + NSUB)
                eb = cs.new(b["Eb"] + NSUB)
                with kloop(t, nblk):
                    W(t, LD.h, ld, 16)
                    with kloop(t, OG // 2):
                        for e in range(2):
                            W(t, WL.h, wl, 16)
                            W(t, EVa.h, ea, NSUB)
                            last = None
                            for kh in range(8):
                                for s_ in range(NSUB):
                                    last = t.matmul(
                                        PS[s_][:], wt[e][:, kh, :],
                                        inb[:, kh,
                                            s_ * 512:(s_ + 1) * 512],
                                        start=(kh == 0), stop=(kh == 7))
                            M1.inc(last, 1)
                            W(t, WL2.h, wl2, 16)
                            W(t, EVb.h, eb, NSUB)
                            last = None
                            for kh in range(8):
                                for s_ in range(NSUB):
                                    last = t.matmul(
                                        PS[4 + s_][:], wt[2 + e][:, kh, :],
                                        inb[:, kh,
                                            s_ * 512:(s_ + 1) * 512],
                                        start=(kh == 0), stop=(kh == 7))
                            M2.inc(last, 1)
                cs.free()

            @blk.scalar
            def _(sc):
                EVa.seed(sc, NSUB)
                EVb.seed(sc, NSUB)
                ST_.seed(sc, 16)
                cs = Cs(sc)
                m1 = cs.new(b["M1"] + 1)
                m2 = cs.new(b["M2"] + 1)
                gl = cs.new(b["GL"] + 1)
                gls = cs.new(b["GL"] + 1 + OG)
                a_of = cs.new(int(av[:, 0:OG, 0:TBW].offset))
                stf = cs.new(b["ST"] + 16)
                with kloop(sc, nblk):
                    W(sc, ST_.h, stf, 16)
                    for o in range(OG):
                        e = o % 2
                        W(sc, M1.h, m1, 1)
                        W(sc, GL.h, gl, 1)
                        last = None
                        for s_ in range(NSUB):
                            last = sc.activation(
                                p1t[:, s_ * 512:(s_ + 1) * 512],
                                PS[s_][:], AF.Silu,
                                scale=s1_sb[:, o:o + 1])
                        EVa.inc(last, NSUB)
                        W(sc, M2.h, m2, 1)
                        last = None
                        for s_ in range(NSUB):
                            last = sc.activation(
                                p2t[:, s_ * 512:(s_ + 1) * 512],
                                PS[4 + s_][:], AF.Copy,
                                scale=s2_sb[:, o:o + 1])
                        EVb.inc(last, NSUB)
                    W(sc, GL.h, gls, OG)
                    ST_.inc(sc.dma_start(
                        out=dyn(av[:, 0:OG, 0:TBW], a_of), in_=ab.ap()), 16)
                    a_of.add(TBW)
                cs.free()
                sc.wait_ge(ST_.h, ST_.total)

            @blk.vector
            def _(v):
                GL.seed(v, 1)
                cs = Cs(v)
                ebv = cs.new(b["Eb"] + 2 * NSUB)
                with kloop(v, nblk):
                    for o in range(OG):
                        W(v, EVb.h, ebv, NSUB)
                        GL.inc(v.tensor_mul(ab[:, o, :], p1t[:], p2t[:]), 1)
                cs.free()
        _pe()

    def P_conv():
        _pb()
        DG, XL, CM, EVc, CC_, ST_ = (sem(f"cv{x}") for x in
                                     ("DG", "XL", "CM", "EV", "CC", "ST"))
        b = dict(DG=DG.total, XL=XL.total, CM=CM.total, EV=EVc.total,
                 CC=CC_.total, ST=ST_.total)
        dgt = sbuf("cvd", [128, 2, KC, 128], BF16)
        xit = sbuf("cvx", [128, 2, PAD + TL], BF16)
        xct = sbuf("cvc", [128, TL], F32)
        xbt = sbuf("cvbt", [128, TL], BF16)
        cwbst = sbuf("cvcw", [128, 2, KC + 1], F32)
        cwbtm = convwbv.ap()[:, 0:1, :]
        xitm = xin_d.ap()[0:1].squeeze(0)
        xctm = xc_d.ap()[0:1].squeeze(0)
        xbtm = xcb_d.ap()[0:1].squeeze(0)
        xcc_eg0 = bass.AP(dmxc_d, 16 * 128 * TCH,
                          [[TCH, 128], [2 * 16 * 128 * TCH, NCH], [1, TCH]])
        xcv2 = xct.ap().rearrange("p (c t) -> p c t", t=TCH)

        with nc.Block() as blk:
            @blk.sync
            def _(sy):
                cs = Cs(sy)
                cmf = cs.new(b["CM"] + 0)
                dgf = cs.new(b["DG"] + 0)
                evs = cs.new(b["EV"] + 4)
                xi_of = cs.new(0)
                cw_of = cs.new(int(cwbtm.offset))
                with kloop(sy, 8):
                    for e in range(2):
                        W(sy, CM.h, cmf, 16)
                        W(sy, DG.h, dgf, 1)
                        XL.inc(sy.dma_start(out=xit[:, e, :],
                                            in_=dyn(xitm, xi_of)), 16)
                        xi_of.add(128 * (PAD + TL))
                        W(sy, EVc.h, evs, 16)
                        XL.inc(sy.dma_start(
                            out=cwbst.ap()[:, e:e + 1, :],
                            in_=dyn(cwbtm, cw_of)), 16)
                        cw_of.add(KC + 1)
                cs.free()

            @blk.gpsimd
            def _(g):
                ST_.seed(g, 48)
                cs = Cs(g)
                ccr = cs.new(b["CC"] + 2)
                xc_of = cs.new(0)
                xb_of = cs.new(0)
                xcc_of = cs.new(int(xcc_eg0.offset))
                with kloop(g, 16):
                    W(g, CC_.h, ccr, 1)
                    ST_.inc(g.dma_start(out=dyn(xctm, xc_of),
                                        in_=xct[:]), 16)
                    xc_of.add(128 * TL)
                    ST_.inc(g.dma_start(out=dyn(xbtm, xb_of),
                                        in_=xbt[:]), 16)
                    xb_of.add(128 * TL)
                    ST_.inc(g.dma_start(out=dyn(xcc_eg0, xcc_of),
                                        in_=xcv2[:]), 16)
                    xcc_of.add(128 * TCH)
                cs.free()
                g.wait_ge(ST_.h, ST_.total)

            @blk.vector
            def _(v):
                CC_.seed(v, 1)
                DG.seed(v, 1)
                cs = Cs(v)
                cmd = cs.new(b["CM"] + 0)
                xlv = cs.new(b["XL"] + 32)
                evr = cs.new(b["EV"] + 36)
                stv = cs.new(b["ST"] + 48)
                with kloop(v, 8):
                    for e in range(2):
                        W(v, CM.h, cmd, 16)
                        W(v, XL.h, xlv, 32)
                        last = None
                        for kk in range(KC):
                            last = v.tensor_scalar(dgt[:, e, kk, :],
                                                   eye_sb[:],
                                                   cwbst[:, e, kk:kk + 1],
                                                   None, ALU.mult)
                        DG.inc(last, 1)
                        W(v, EVc.h, evr, 16)
                        W(v, ST_.h, stv, 48)
                        CC_.inc(v.tensor_copy(xbt[:], xct[:]), 1)
                cs.free()

            @blk.tensor
            def _(t):
                CM.seed(t, 16)
                cs = Cs(t)
                dg = cs.new(b["DG"] + 2)
                xl = cs.new(b["XL"] + 32)
                ev = cs.new(b["EV"] + 17)
                with kloop(t, 8):
                    for e in range(2):
                        W(t, DG.h, dg, 1)
                        W(t, XL.h, xl, 32)
                        for ch in range(NCH):
                            W(t, EVc.h, ev, 1)
                            last = None
                            for kk in range(KC):
                                last = t.matmul(
                                    PS[ch % 4][:], dgt[:, e, kk, :],
                                    xit[:, e,
                                        ch * TCH + kk:ch * TCH + kk + TCH],
                                    start=(kk == 0), stop=(kk == KC - 1))
                            CM.inc(last, 1)
                cs.free()

            @blk.scalar
            def _(sc):
                EVc.seed(sc, 20)
                cs = Cs(sc)
                sts = cs.new(b["ST"] + 48)
                cm = cs.new(b["CM"] + 16 + 1)
                with kloop(sc, 8):
                    for e in range(2):
                        W(sc, ST_.h, sts, 48)
                        for ch in range(NCH):
                            W(sc, CM.h, cm, 1)
                            EVc.inc(sc.activation(
                                xct[:, ch * TCH:(ch + 1) * TCH],
                                PS[ch % 4][:], AF.Silu,
                                bias=cwbst[:, e, KC:KC + 1]), 1)
                cs.free()
        _pe()

    def P_wxdt():
        _pb()
        LD, MX, EX, MD, ED_, DV, ST_ = (sem(f"wx{x}") for x in
                                        ("LD", "MX", "EX", "MD", "ED", "DV",
                                         "ST"))
        b = dict(LD=LD.total, MX=MX.total, EX=EX.total, MD=MD.total,
                 ED=ED_.total, DV=DV.total, ST=ST_.total)
        wdt_sb = sbuf("wdt_sb", [RK, 16, 128], BF16)
        wx_sb = sbuf("wx_sb", [128, 16, 128], BF16)
        xbt = sbuf("wxxb", [128, 16, TCH], BF16)
        mkt = sbuf("wxmk", [128, TCH], F32)
        dtl = sbuf("wxdt1", [RK, TCH], BF16)
        bct = sbuf("wxbc", [64, TCH], F32)
        t2b = sbuf("wxt2", [128, 16, TCH], F32)
        t4b = sbuf("wxt4", [128, 16, TCH], F32)
        dmt = sbuf("wxdm", [128, 16, TCH], F32)
        xbv = xcb_d.ap().rearrange("g p t -> p g t")
        mktm = maskv.ap()[0:1, 0:TCH]
        dm2tm = bass.AP(dmxc_d, 0, [[TCH, 128], [128 * TCH, 16], [1, TCH]])
        bctm = bc_d.ap()[:, 0:TCH]

        with nc.Block() as blk:
            @blk.gpsimd
            def _(g):
                LD.inc(g.dma_start(
                    out=wdt_sb[:],
                    in_=wdtT.ap().rearrange("r (og p) -> r og p", p=128)),
                    16)
                LD.inc(g.dma_start(
                    out=wx_sb[:],
                    in_=wxT.ap().rearrange("(eg p) o -> p eg o", p=128)), 16)

            @blk.sync
            def _(sy):
                ST_.seed(sy, 32)
                cs = Cs(sy)
                dvf = cs.new(b["DV"] + 32)
                dvr = cs.new(b["DV"] + 48)
                exr = cs.new(b["EX"] + 2 + 2)
                xb_of = cs.new(int(xbv[:, :, 0:TCH].offset))
                mk_of = cs.new(int(mktm.offset))
                dm_of = cs.new(0)
                bo_of = cs.new(int(bctm.offset))
                with kloop(sy, NCH):
                    W(sy, DV.h, dvf, 16)
                    LD.inc(sy.dma_start(out=xbt[:],
                                        in_=dyn(xbv[:, :, 0:TCH], xb_of)),
                           16)
                    xb_of.add(TCH)
                    LD.inc(sy.dma_start(
                        out=mkt[:],
                        in_=dyn(mktm, mk_of)
                        .partition_broadcast(128).squeeze(1)), 16)
                    mk_of.add(TCH)
                    W(sy, DV.h, dvr, 16)
                    ST_.inc(sy.dma_start(out=dyn(dm2tm, dm_of),
                                         in_=dmt[:]), 16)
                    dm_of.add(2 * 16 * 128 * TCH)
                    W(sy, EX.h, exr, 2)
                    ST_.inc(sy.dma_start(out=dyn(bctm, bo_of),
                                         in_=bct[0:32, :]), 16)
                    bo_of.add(TCH)
                cs.free()
                sy.wait_ge(ST_.h, ST_.total)

            @blk.tensor
            def _(t):
                cs = Cs(t)
                ld = cs.new(b["LD"] + 64)
                exf = cs.new(b["EX"] + 2)
                exd = cs.new(b["EX"] + 4)
                ed = cs.new(b["ED"] + 1)
                with kloop(t, NCH):
                    W(t, LD.h, ld, 32)
                    W(t, EX.h, exf, 2)
                    last = None
                    for kg in range(16):
                        last = t.matmul(PS[4][:], wx_sb[:, kg, :],
                                        xbt[:, kg, :], start=(kg == 0),
                                        stop=(kg == 15))
                    MX.inc(last, 1)
                    W(t, EX.h, exd, 2)
                    for o in range(16):
                        W(t, ED_.h, ed, 1)
                        MD.inc(t.matmul(PS[5 + o % 2][:], wdt_sb[:, o, :],
                                        dtl[:], start=True, stop=True), 1)
                cs.free()

            @blk.scalar
            def _(sc):
                EX.seed(sc, 2)
                ED_.seed(sc, 2)
                cs = Cs(sc)
                mx = cs.new(b["MX"] + 1)
                stb = cs.new(b["ST"] + 32)
                dvt = cs.new(b["DV"] + 32)
                md = cs.new(b["MD"] + 1)
                with kloop(sc, NCH):
                    W(sc, MX.h, mx, 1)
                    W(sc, ST_.h, stb, 32)
                    sc.activation(dtl[:], PS[4][0:64, :], AF.Copy)
                    EX.inc(sc.activation(bct[:], PS[4][64:128, :],
                                         AF.Copy), 2)
                    W(sc, DV.h, dvt, 16)
                    for o in range(16):
                        W(sc, MD.h, md, 1)
                        sc.activation(t2b[:, o, :], PS[5 + o % 2][:],
                                      AF.Abs, bias=bdt_sb[:, o:o + 1])
                        sc.activation(t2b[:, o, :], t2b[:, o, :], AF.Exp,
                                      scale=np.float32(-1.0))
                        sc.activation(t2b[:, o, :], t2b[:, o, :], AF.Ln,
                                      bias=np.float32(1.0))
                        ED_.inc(sc.activation(t4b[:, o, :],
                                              PS[5 + o % 2][:], AF.Relu,
                                              bias=bdt_sb[:, o:o + 1]), 1)
                cs.free()

            @blk.vector
            def _(v):
                DV.seed(v, 32)
                cs = Cs(v)
                stv = cs.new(b["ST"] + 32)
                edv = cs.new(b["ED"] + 2 + 1)
                with kloop(v, NCH):
                    W(v, ST_.h, stv, 32)
                    for o in range(16):
                        W(v, ED_.h, edv, 1)
                        v.tensor_add(dmt[:, o, :], t2b[:, o, :],
                                     t4b[:, o, :])
                        DV.inc(v.tensor_mul(dmt[:, o, :], dmt[:, o, :],
                                            mkt[:]), 1)
                cs.free()
        _pe()

    def P_scan(part):
        _pb()
        tag = f"sn{part}"
        LD, BR, DA, VF, YD, ST_, VN = (sem(f"{tag}{x}") for x in
                                       ("LD", "BR", "DA", "VF", "YD", "ST",
                                        "VN"))
        b = dict(LD=LD.total, BR=BR.total, DA=DA.total, VF=VF.total,
                 YD=YD.total, ST=ST_.total, VN=VN.total)
        with_y = part == "b"
        dxb = sbuf(f"{tag}dxb", [128, 2, 16, TCH], F32)
        ust = sbuf(f"{tag}u", [128, TCH], F32)
        dab = sbuf(f"{tag}da", [128, 2, TCH], F32)
        dbx = sbuf(f"{tag}db", [128, TCH], F32)
        hti = sbuf(f"{tag}h", [128, TCH], F32)
        brep = sbuf(f"{tag}br", [128, NS, TCH], F32)
        crep = sbuf(f"{tag}cr", [128, NS, TCH], F32) if with_y else None
        ybig = sbuf(f"{tag}y", [128, 16, TCH], F32) if with_y else None
        coff = PO if with_y else 0
        ch0 = NPO if with_y else 0
        nbr = 32 if with_y else 16
        dmctm = bass.AP(dmxc_d, ch0 * 2 * 16 * 128 * TCH,
                        [[TCH, 128], [16 * 128 * TCH, 2],
                         [128 * TCH, 16], [1, TCH]])
        yctm = bass.AP(y_d, 0, [[TCH, 128], [128 * TCH, 16], [1, TCH]])
        btm = bc_d.ap()[0:16, coff:coff + TCH]
        ctm = bc_d.ap()[16:32, coff:coff + TCH]

        with nc.Block() as blk:
            @blk.sync
            def _(sy):
                ST_.seed(sy, 16)
                cs = Cs(sy)
                vfb = cs.new(b["VF"] + 16)
                dm_of = cs.new(int(dmctm.offset))
                yd_ = cs.new(b["YD"] + 16) if with_y else None
                y_of = cs.new(0) if with_y else None
                with kloop(sy, 8):
                    W(sy, VF.h, vfb, 16)
                    LD.inc(sy.dma_start(out=dxb[:], in_=dyn(dmctm, dm_of)),
                           16)
                    dm_of.add(2 * 16 * 128 * TCH)
                    if with_y:
                        W(sy, YD.h, yd_, 16)
                        ST_.inc(sy.dma_start(out=dyn(yctm, y_of),
                                             in_=ybig[:]), 16)
                        y_of.add(16 * 128 * TCH)
                cs.free()
                if with_y:
                    sy.wait_ge(ST_.h, ST_.total)

            @blk.scalar
            def _(sc):
                cs = Cs(sc)
                ld = cs.new(b["LD"] + 16)
                vnn = cs.new(b["VN"] + 1)
                vfbr = cs.new(b["VF"] + 16)
                br_of = cs.new(int(btm.offset))
                cr_of = cs.new(int(ctm.offset)) if with_y else None
                with kloop(sc, 8):
                    W(sc, LD.h, ld, 16)
                    W(sc, VF.h, vfbr, 16)
                    BR.inc(sc.dma_start(
                        out=brep[:],
                        in_=dyn(btm, br_of).partition_broadcast(128)), 16)
                    br_of.add(TCH)
                    if with_y:
                        BR.inc(sc.dma_start(
                            out=crep[:],
                            in_=dyn(ctm, cr_of)
                            .partition_broadcast(128)), 16)
                        cr_of.add(TCH)
                    for eg in range(16):
                        for nn in range(NS):
                            W(sc, VN.h, vnn, 1)
                            DA.inc(sc.activation(
                                dab[:, nn % 2, :], dxb[:, 0, eg, :], AF.Exp,
                                scale=A_sb[:, eg * NS + nn:
                                           eg * NS + nn + 1]), 1)
                cs.free()

            @blk.vector
            def _(v):
                VF.seed(v, 16)
                VN.seed(v, 2)
                cs = Cs(v)
                br = cs.new(b["BR"] + nbr)
                da = cs.new(b["DA"] + 1)
                stv = cs.new(b["ST"] + 16) if with_y else None
                with kloop(v, 8):
                    W(v, BR.h, br, nbr)
                    if with_y:
                        W(v, ST_.h, stv, 16)
                    for eg in range(16):
                        v.tensor_mul(ust[:], dxb[:, 0, eg, :],
                                     dxb[:, 1, eg, :])
                        last_c = None
                        last_y = None
                        for nn in range(NS):
                            cl = eg * NS + nn
                            W(v, DA.h, da, 1)
                            v.tensor_mul(dbx[:], ust[:], brep[:, nn, :])
                            sci = v.tensor_tensor_scan(
                                hti[:], dab[:, nn % 2, :], dbx[:],
                                carry[:, cl:cl + 1], ALU.mult, ALU.add)
                            VN.inc(sci, 1)
                            last_c = v.tensor_copy(carry[:, cl:cl + 1],
                                                   hti[:, TCH - 1:TCH])
                            if with_y:
                                cm_ = v.tensor_mul(hti[:], hti[:],
                                                   crep[:, nn, :])
                                if nn == NS - 1:
                                    VF.inc(cm_, 1)
                                if nn == 0:
                                    last_y = v.tensor_copy(
                                        ybig[:, eg, :], hti[:])
                                else:
                                    last_y = v.tensor_add(
                                        ybig[:, eg, :], ybig[:, eg, :],
                                        hti[:])
                        if with_y:
                            YD.inc(last_y, 1)
                        else:
                            VF.inc(last_c, 1)
                cs.free()
        _pe()

    def P_gate():
        _pb()
        LD, SZ, DV, ST_ = (sem(f"gt{x}") for x in ("LD", "SZ", "DV", "ST"))
        b = dict(LD=LD.total, SZ=SZ.total, DV=DV.total, ST=ST_.total)
        yt = sbuf("gty", [128, 16, TCH], F32)
        xct = sbuf("gtxc", [128, 16, TCH], F32)
        zt = sbuf("gtz", [128, 16, TCH], F32)
        szt = sbuf("gtsz", [128, 16, TCH], F32)
        ygt = sbuf("gtyg", [128, 16, TCH], BF16)
        yctm = bass.AP(y_d, 0, [[TCH, 128], [128 * TCH, 16], [1, TCH]])
        xcv = xc_d.ap().rearrange("g p t -> p g t")
        zv = z_d.ap().rearrange("g p t -> p g t")
        ygv = yg_d.ap().rearrange("g p t -> p g t")

        with nc.Block() as blk:
            @blk.scalar
            def _(sc):
                ST_.seed(sc, 16)
                cs = Cs(sc)
                dvf = cs.new(b["DV"] + 32)
                dvr = cs.new(b["DV"] + 64)
                y_of = cs.new(0)
                xc_of = cs.new(int(xcv[:, :, PO:PO + TCH].offset))
                z_of = cs.new(int(zv[:, :, PO:PO + TCH].offset))
                yg_of = cs.new(int(ygv[:, :, 0:TCH].offset))
                ld = cs.new(b["LD"] + 48)
                dvs = cs.new(b["DV"] + 32)
                with kloop(sc, NPO):
                    W(sc, DV.h, dvf, 32)
                    LD.inc(sc.dma_start(out=yt[:], in_=dyn(yctm, y_of)), 16)
                    y_of.add(16 * 128 * TCH)
                    LD.inc(sc.dma_start(
                        out=xct[:],
                        in_=dyn(xcv[:, :, 0:TCH], xc_of)), 16)
                    xc_of.add(TCH)
                    LD.inc(sc.dma_start(out=zt[:],
                                        in_=dyn(zv[:, :, 0:TCH], z_of)), 16)
                    z_of.add(TCH)
                    W(sc, LD.h, ld, 48)
                    W(sc, DV.h, dvs, 32)
                    last = None
                    for eg in range(16):
                        last = sc.activation(szt[:, eg, :], zt[:, eg, :],
                                             AF.Silu)
                    SZ.inc(last, 1)
                    W(sc, DV.h, dvr, 32)
                    ST_.inc(sc.dma_start(out=dyn(ygv[:, :, 0:TCH], yg_of),
                                         in_=ygt[:]), 16)
                    yg_of.add(TCH)
                cs.free()
                sc.wait_ge(ST_.h, ST_.total)

            @blk.vector
            def _(v):
                DV.seed(v, 32)
                cs = Cs(v)
                sz = cs.new(b["SZ"] + 1)
                stv = cs.new(b["ST"] + 16)
                with kloop(v, NPO):
                    W(v, SZ.h, sz, 1)
                    W(v, ST_.h, stv, 16)
                    for eg in range(16):
                        DV.inc(v.scalar_tensor_tensor(
                            yt[:, eg, :], xct[:, eg, :],
                            dpar_sb[:, eg:eg + 1], yt[:, eg, :],
                            ALU.mult, ALU.add), 1)
                        DV.inc(v.tensor_mul(ygt[:, eg, :], yt[:, eg, :],
                                            szt[:, eg, :]), 1)
                cs.free()
        _pe()

    P_ln("l1", x_f, xn1_d, TL, False)
    P_ffn_in("fa", xn1_d, a_d, wview("w1b"), wview("w2b"), TL)
    P_stream("fb", a_d, 32, wview("h1b"), 32, bb_d, TL, AF.Gelu,
             scale_sb=sh1_sb)
    P_stream("fc", bb_d, 32, wview("h2b"), 32, cc_d, TL, AF.Gelu,
             scale_sb=sh2_sb)
    P_stream("fd", cc_d, 32, wview("w3b"), 8, r1_d, TL, AF.Copy,
             scale_sb=s3h_sb, out_dt=F32, dve="res", res_d=x_f)
    if stage != 1:
        P_ln("l2", r1_d, xn2_d, TL, True, sum_out=lnr_d)
        P_stream("wxp", xn2_d, 8, wview("winb"), 16, xin_d, TL,
                 AF.Identity, bias_sb=binx_sb, out_dt=BF16, dve="mask",
                 store_off=PAD)
        P_stream("wzp", xn2_d, 8, wview("winb"), 16, z_d, TL,
                 AF.Identity, bias_sb=binz_sb, out_dt=F32, wcol_off=2048)
        P_conv()
        P_wxdt()
        if stage != 2:
            P_scan("a")
            P_scan("b")
            P_gate()
            P_stream("wop", yg_d, 16, wview("woutb"), 8, r2_d, TPOST,
                     AF.Identity, bias_sb=bout_sb, out_dt=F32, dve="res",
                     res_d=lnr_d, res_off=PO)
            if stage != 3:
                P_ln("l3", r2_d, xn3_d, TPOST, False)
                P_ffn_in("f2", xn3_d, a2_d, wview("w1b"), wview("w2b"),
                         TPOST)
                P_stream("g2", a2_d, 32, wview("h1b"), 32, bb2_d, TPOST,
                         AF.Gelu, scale_sb=sh1_sb)
                P_stream("g3", bb2_d, 32, wview("h2b"), 32, cc2_d, TPOST,
                         AF.Gelu, scale_sb=sh2_sb)
                P_stream("g4", cc2_d, 32, wview("w3b"), 8, out_f, TPOST,
                         AF.Copy, scale_sb=s3h_sb, out_dt=F32, dve="res",
                         res_d=r2_d)
    return nc, st


def _vec_pg(v, G):
    return np.ascontiguousarray(np.asarray(v, np.float32).reshape(G, 128).T)


def prep_inputs(I):
    x = np.asarray(I["L0_attn_x"], np.float32)
    w = {k: np.asarray(v, np.float32) for k, v in I.items()}
    com = {}
    com["wall"] = np.concatenate([
        np.ascontiguousarray(w["L0_ffn_w1_w"].T).ravel(),
        np.ascontiguousarray(w["L0_ffn_w2_w"].T).ravel(),
        np.ascontiguousarray(w["L0_ffn_h1_w"].T).ravel(),
        np.ascontiguousarray(w["L0_ffn_h2_w"].T).ravel(),
        np.ascontiguousarray(w["L0_ffn_w3_w"].T).ravel(),
        np.ascontiguousarray(w["L0_attn_W_in"].T).ravel(),
        np.ascontiguousarray(w["L0_attn_W_out"].T).ravel()])
    wxp = np.zeros((ED, 128), np.float32)
    wxp[:, 0:96] = w["L0_attn_W_x"].T
    com["wxT"] = wxp
    com["wdtT"] = np.ascontiguousarray(w["L0_attn_W_dt"].T)
    com["gammav"] = _vec_pg(w["L0_attn_gamma"], 8)
    com["betav"] = _vec_pg(w["L0_attn_beta"], 8)
    com["bdtv"] = _vec_pg(w["L0_attn_b_dt"], 16)
    com["dparv"] = _vec_pg(w["L0_attn_D_param"], 16)
    com["alogv"] = np.ascontiguousarray(
        w["L0_attn_A_log"].reshape(16, 128, NS).transpose(1, 0, 2)
        .reshape(128, 16 * NS))
    cwb = np.zeros((128, 16, KC + 1), np.float32)
    cwb[:, :, 0:KC] = w["L0_attn_conv_w"].reshape(16, 128, KC).transpose(
        1, 0, 2)
    cwb[:, :, KC] = w["L0_attn_conv_b"].reshape(16, 128).T
    com["convwbv"] = cwb
    com["eyev"] = np.eye(128, dtype=np.float32)
    com["s1v"] = _vec_pg(w["L0_ffn_w1_s"], 32)
    com["s2v"] = _vec_pg(w["L0_ffn_w2_s"], 32)
    com["sh1v"] = _vec_pg(w["L0_ffn_h1_s"], 32)
    com["sh2v"] = _vec_pg(w["L0_ffn_h2_s"], 32)
    com["s3v"] = _vec_pg(w["L0_ffn_w3_s"], 8)
    com["binxv"] = _vec_pg(w["L0_attn_b_in"][:ED], 16)
    com["binzv"] = _vec_pg(w["L0_attn_b_in"][ED:], 16)
    com["boutv"] = _vec_pg(w["L0_attn_b_out"], 8)

    in_maps = []
    for c in range(8):
        b_, half = c // 2, c % 2
        if half == 1:
            xf = np.ascontiguousarray(x[b_].T)
            mk = np.ones((1, TL), np.float32)
        else:
            xf = np.concatenate(
                [np.zeros((D, TPOST), np.float32),
                 np.ascontiguousarray(x[b_, :TPOST].T)], axis=1)
            mk = np.concatenate(
                [np.zeros((1, TPOST), np.float32),
                 np.ones((1, TL - TPOST), np.float32)], axis=1)
        m = dict(com)
        m["x_f"] = np.ascontiguousarray(xf.reshape(8, 128, TL))
        m["maskv"] = np.ascontiguousarray(mk)
        in_maps.append(m)
    return in_maps


def kernel(**inputs):
    nc, st = build(stage=0)
    in_maps = prep_inputs(inputs)
    try:
        res = run_bass_kernel_spmd(nc, in_maps, core_ids=list(range(8)))
    finally:
        st.close()
    out = np.empty((4, 8192, D), np.float32)
    for c in range(8):
        b_, half = c // 2, c % 2
        of = res.results[c]["out_f"].reshape(D, TPOST)
        out[b_, half * TPOST:(half + 1) * TPOST, :] = of.T
    return out


# version 06
# speedup vs baseline: 1.1455x; 1.1455x over previous
import os
os.environ.setdefault("BASS_DISABLE_FRAME_TO_TRACEBACK", "1")

import contextlib
import numpy as np
import concourse.bass as bass
import concourse.mybir as mybir
from concourse.bass_utils import run_bass_kernel_spmd
from concourse.ordered_engine_set import OrderedEngineSet

F32 = mybir.dt.float32
BF16 = mybir.dt.bfloat16
AF = mybir.ActivationFunctionType
ALU = mybir.AluOpType

D, H, ED, NS, KC, RK = 1024, 4096, 2048, 16, 15, 64
TL, TPOST, TCH, PAD, EPS = 8192, 4096, 512, 14, 1e-5
PO = TL - TPOST
NCH = TL // TCH
NPO = TPOST // TCH

_MULT = [1]


class _Sem:
    def __init__(self, h):
        self.h = h
        self.total = 0

    def inc(self, instr, n=1):
        instr.then_inc(self.h, n)
        self.total += n * _MULT[0]
        return instr

    def seed(self, eng, n):
        eng.sem_inc(self.h, n)
        self.total += n


def build(stage=0, debug_outputs=()):
    nc = bass.Bass("TRN2", target_bir_lowering=False)
    st = contextlib.ExitStack()
    _pool = [_Sem(st.enter_context(nc.semaphore(f"sp{i}"))) for i in range(18)]
    _pool_i = [0]

    def sem(name):
        s = _pool[_pool_i[0] % len(_pool)]
        _pool_i[0] += 1
        return s

    _ph = [None]

    def _pb():
        _ph[0] = contextlib.ExitStack()

    def _pe():
        _ph[0].close()
        _ph[0] = None

    def sbuf(name, shape, dt):
        stk = _ph[0] if _ph[0] is not None else st
        return stk.enter_context(nc.sbuf_tensor(name, shape, dt))

    psum = lambda name, shape: st.enter_context(
        nc.psum_tensor(name, shape, F32))

    @contextlib.contextmanager
    def kloop(eng, n):
        engines = OrderedEngineSet([eng.engine])
        name = f"kl_{nc.next_id()}"
        regs = nc.alloc_registers(f"{name}_i", engines=engines)
        nc.regs_mov(regs, 0)
        nc.br(f"{name}_loop", engines=engines)
        _MULT[0] *= n
        with nc.body(f"{name}_loop", valid_engines=engines):
            yield None
            nc.regs_alu(regs, regs, 1, op=ALU.add)
            nc.br_lt(regs, n, on_true=f"{name}_loop",
                     on_false=f"{name}_end", engines=engines)
        _MULT[0] //= n
        nc.switch_bb(f"{name}_end")
        for h in regs.handles:
            nc.free_register(h)

    class C:

        _n = [0]

        def __init__(self, eng, init):
            C._n[0] += 1
            self.eng = eng
            self.reg = eng.alloc_register(f"c{C._n[0]}")
            eng.reg_mov(self.reg, int(init))

        def add(self, step):
            if step:
                self.eng.reg_alu(self.reg, self.reg, int(step), op=ALU.add)

        def free(self):
            self.eng.free_register(self.reg)

    class Cs:

        def __init__(self, eng):
            self.eng = eng
            self.lst = []

        def new(self, init):
            c = C(self.eng, init)
            self.lst.append(c)
            return c

        def free(self):
            for c in self.lst:
                c.free()
            self.lst.clear()

    def dyn(tmpl_ap, ctr):
        return bass.AP(tmpl_ap.tensor, ctr.reg,
                       [list(p) for p in tmpl_ap.ap])

    def W(eng, semh, ctr, step):
        eng.wait_ge(semh, ctr.reg)
        ctr.add(step)

    def reflect(eng, ctr, ksum):
        eng.reg_alu(ctr.reg, ctr.reg, int(ksum), op=ALU.subtract)
        eng.reg_alu(ctr.reg, ctr.reg, -1, op=ALU.mult)

    EI = lambda n, s, d=F32: nc.dram_tensor(n, s, d, kind="ExternalInput")
    x_f = EI("x_f", [8, 128, TL])
    maskv = EI("maskv", [1, TL])
    NWALL = 2 * D * H + 2 * H * H + H * D + D * 2 * ED + ED * D
    wall = EI("wall", [NWALL])
    wxT = EI("wxT", [ED, 128])
    wdtT = EI("wdtT", [RK, ED])
    gammav, betav = EI("gammav", [128, 8]), EI("betav", [128, 8])
    bdtv = EI("bdtv", [128, 16])
    dparv = EI("dparv", [128, 16])
    alogv = EI("alogv", [128, 256])
    eyev = EI("eyev", [128, 128])
    convwbv = EI("convwbv", [128, 16, KC + 1])
    s1v, s2v = EI("s1v", [128, 32]), EI("s2v", [128, 32])
    sh1v, sh2v = EI("sh1v", [128, 32]), EI("sh2v", [128, 32])
    s3v = EI("s3v", [128, 8])
    binxv, binzv = EI("binxv", [128, 16]), EI("binzv", [128, 16])
    boutv = EI("boutv", [128, 8])

    out_f = nc.dram_tensor("out_f", [8, 128, TPOST], F32,
                           kind="ExternalOutput")

    def idram(name, shape, dt):
        kind = "ExternalOutput" if name in debug_outputs else "Internal"
        return nc.dram_tensor(name, shape, dt, kind=kind)

    wallb = idram("wallb", [NWALL], BF16)
    _woff = {}
    _o = 0
    for _nm, _r, _c in [("w1b", D, H), ("w2b", D, H), ("h1b", H, H),
                        ("h2b", H, H), ("w3b", H, D), ("winb", D, 2 * ED),
                        ("woutb", ED, D)]:
        _woff[_nm] = (_o, _r, _c)
        _o += _r * _c
    assert _o == NWALL

    def wview(nm):
        off, r, c = _woff[nm]
        return wallb.ap()[off:off + r * c].rearrange(
            "(kg p o) -> p kg o", p=128, o=c)

    xn1_d = idram("xn1_d", [8, 128, TL], BF16)
    a_d = idram("a_d", [32, 128, TL], BF16)
    bb_d = idram("bb_d", [32, 128, TL], BF16)
    cc_d = idram("cc_d", [32, 128, TL], BF16)
    r1_d = idram("r1_d", [8, 128, TL], F32)
    lnr_d = idram("lnr_d", [8, 128, TL], F32)
    xn2_d = idram("xn2_d", [8, 128, TL], BF16)
    xin_d = idram("xin_d", [16, 128, PAD + TL], BF16)
    z_d = idram("z_d", [16, 128, TL], F32)
    xc_d = idram("xc_d", [16, 128, TL], F32)
    dmxc_d = idram("dmxc_d", [NCH, 2, 16, 128, TCH], F32)
    xcb_d = idram("xcb_d", [16, 128, TL], BF16)
    bc_d = idram("bc_d", [2 * NS, TL], F32)
    y_d = idram("y_d", [NPO, 16, 128, TCH], F32)
    yg_d = idram("yg_d", [16, 128, TPOST], BF16)
    r2_d = idram("r2_d", [8, 128, TPOST], F32)
    xn3_d = idram("xn3_d", [8, 128, TPOST], BF16)
    a2_d = idram("a2_d", [32, 128, TPOST], BF16)
    bb2_d = idram("bb2_d", [32, 128, TPOST], BF16)
    cc2_d = idram("cc2_d", [32, 128, TPOST], BF16)

    gam_sb = sbuf("gam_sb", [128, 8], F32)
    bet_sb = sbuf("bet_sb", [128, 8], F32)
    bdt_sb = sbuf("bdt_sb", [128, 16], F32)
    dpar_sb = sbuf("dpar_sb", [128, 16], F32)
    A_sb = sbuf("A_sb", [128, 256], F32)
    s1_sb = sbuf("s1_sb", [128, 32], F32)
    s2_sb = sbuf("s2_sb", [128, 32], F32)
    sh1_sb = sbuf("sh1_sb", [128, 32], F32)
    sh2_sb = sbuf("sh2_sb", [128, 32], F32)
    binx_sb = sbuf("binx_sb", [128, 16], F32)
    binz_sb = sbuf("binz_sb", [128, 16], F32)
    bout_sb = sbuf("bout_sb", [128, 8], F32)
    ones_c = sbuf("ones_c", [128, 1], F32)
    ones_r = sbuf("ones_r", [1, 128], F32)
    eye_sb = sbuf("eye_sb", [128, 128], F32)
    carry = sbuf("carry", [128, 256], F32)
    s3h_sb = sbuf("s3h_sb", [128, 8], F32)

    PS = [psum(f"PS{i}", [128, 512]) for i in range(8)]

    L0 = sem("L0")
    with nc.Block() as blk:
        @blk.gpsimd
        def _(g):
            for dst, src in [(gam_sb, gammav), (bet_sb, betav),
                             (bdt_sb, bdtv), (dpar_sb, dparv),
                             (A_sb, alogv), (eye_sb, eyev),
                             (s1_sb, s1v), (s2_sb, s2v),
                             (sh1_sb, sh1v), (sh2_sb, sh2v),
                             (binx_sb, binxv), (binz_sb, binzv),
                             (bout_sb, boutv)]:
                L0.inc(g.dma_start(out=dst[:], in_=src.ap()), 16)
            L0.inc(g.dma_start(out=s3h_sb[:], in_=s3v.ap()), 16)

        @blk.vector
        def _(v):
            v.memset(ones_c[:], 1.0)
            v.memset(ones_r[:], 1.0)
            v.memset(carry[:], 0.0)
            v.wait_ge(L0.h, L0.total)
            v.tensor_scalar(s3h_sb[:], s3h_sb[:], 0.5, None, ALU.mult)

        @blk.scalar
        def _(sc):
            sc.wait_ge(L0.h, L0.total)
            sc.activation(A_sb[:], A_sb[:], AF.Exp)

    with nc.Block() as blk:
        @blk.vector
        def _(v):
            v.tensor_scalar(A_sb[:], A_sb[:], -1.0, None, ALU.mult)


    WCa, WSa = sem("WCa"), sem("WSa")
    _pb()
    ct2 = sbuf("ct2", [128, 2, 4096], BF16)
    NCST = NWALL // (128 * 4096)
    wsv = wall.ap().rearrange("(c p t) -> c p t", p=128, t=4096)
    wdv = wallb.ap().rearrange("(c p t) -> c p t", p=128, t=4096)
    ctv = [ct2.ap()[:, e:e + 1].squeeze(1) for e in (0, 1)]
    with nc.Block() as blk:
        @blk.gpsimd
        def _(g):
            WSa.seed(g, 32)
            cs = Cs(g)
            wst = cs.new(16)
            off = cs.new(0)
            with kloop(g, NCST // 2):
                for e in range(2):
                    W(g, WSa.h, wst, 16)
                    WCa.inc(g.dma_start(
                        out=ctv[e],
                        in_=dyn(wsv[0:1].squeeze(0), off)), 16)
                    off.add(128 * 4096)
            cs.free()

        @blk.sync
        def _(sy):
            cs = Cs(sy)
            wct = cs.new(16)
            off = cs.new(0)
            with kloop(sy, NCST // 2):
                for e in range(2):
                    W(sy, WCa.h, wct, 16)
                    WSa.inc(sy.dma_start(
                        out=dyn(wdv[0:1].squeeze(0), off),
                        in_=ctv[e]), 16)
                    off.add(128 * 4096)
            cs.free()
            sy.wait_ge(WSa.h, WSa.total)

    with nc.Block() as blk:
        @blk.vector
        def _(v):
            v.memset(ct2[:, 0, :], 0.0)

    ZP = sem("ZP")
    with nc.Block() as blk:
        @blk.sync
        def _(sy):
            ZP.inc(sy.dma_start(
                out=xin_d.ap()[:, :, 0:PAD].transpose([1, 0, 2]),
                in_=ct2.ap()[:, 0, 0:16 * PAD].rearrange(
                    "p (g c) -> p g c", g=16)), 16)
            sy.wait_ge(ZP.h, ZP.total)
    _pe()

    def P_ln(tag, src_d, xn_out, T, double, sum_out=None):
        _pb()
        nch = T // TCH
        R = 2 if double else 1
        apc = R + (1 if double else 0)
        hs = sum_out is not None
        LD, SQ, SM, EV1, DV1, EV2, MR, DV2, BC, AP_, ST_ = (
            sem(f"{tag}{x}") for x in
            ("LD", "SQ", "SM", "E1", "D1", "E2", "MR", "D2", "BC", "AP",
             "ST"))
        b = dict(LD=LD.total, SQ=SQ.total, SM=SM.total, E1=EV1.total,
                 D1=DV1.total, E2=EV2.total, MR=MR.total, D2=DV2.total,
                 BC=BC.total, AP=AP_.total, ST=ST_.total)
        xt1 = sbuf(f"{tag}x1", [128, 8, TCH], F32)
        xt = [xt1, xt1]
        sqt = sbuf(f"{tag}sq", [128, 8, TCH], F32)
        u1t = sbuf(f"{tag}u1", [128, 8, TCH], F32)
        nt = sbuf(f"{tag}nt", [128, 8, TCH], F32)
        xnt1 = sbuf(f"{tag}xn1s", [128, 8, TCH], BF16)
        xnt = [xnt1, xnt1]
        rma = sbuf(f"{tag}ra", [1, TCH], F32)
        rmb = sbuf(f"{tag}rb", [1, TCH], F32)
        rmm = sbuf(f"{tag}rm", [1, TCH], F32)
        rmc = sbuf(f"{tag}rc", [1, TCH], F32)
        rmr = sbuf(f"{tag}rr", [1, TCH], F32)
        srcv = src_d.ap().rearrange("g p t -> p g t")
        dstv = xn_out.ap().rearrange("g p t -> p g t")
        if hs:
            SR = sem(f"{tag}SR")
            b2 = dict(SR=SR.total)
            sumv = sum_out.ap().rearrange("g p t -> p g t")

        with nc.Block() as blk:
            @blk.sync
            def _(sy):
                ST_.seed(sy, 16)
                cs = Cs(sy)
                if double:
                    apf = cs.new(b["AP"] + apc)
                    apr = cs.new(b["AP"] + 2 * apc)
                else:
                    apf = cs.new(b["D2"] + 0)
                    apr = cs.new(b["D2"] + 1)
                aph = AP_.h if double else DV2.h
                stp = apc if double else 1
                aso = cs.new(int(srcv[:, :, 0:TCH].offset))
                ado = cs.new(int(dstv[:, :, 0:TCH].offset))
                so_of = cs.new(int(sumv[:, :, 0:TCH].offset)) if hs else None
                with kloop(sy, nch):
                    W(sy, aph, apf, stp)
                    LD.inc(sy.dma_start(
                        out=xt1[:],
                        in_=dyn(srcv[:, :, 0:TCH], aso)), 16)
                    aso.add(TCH)
                    W(sy, aph, apr, stp)
                    ST_.inc(sy.dma_start(
                        out=dyn(dstv[:, :, 0:TCH], ado),
                        in_=xnt1[:]), 16)
                    ado.add(TCH)
                    if hs:
                        SR.inc(sy.dma_start(
                            out=dyn(sumv[:, :, 0:TCH], so_of),
                            in_=sqt.ap()), 16)
                        so_of.add(TCH)
                cs.free()
                sy.wait_ge(ST_.h, ST_.total)
                if hs:
                    sy.wait_ge(SR.h, SR.total)

            @blk.scalar
            def _(sc):
                if double:
                    AP_.seed(sc, apc)
                cs = Cs(sc)
                ld = cs.new(b["LD"] + 16)
                dv2r = cs.new(b["D2"] + 1) if double else None
                sm = cs.new(b["SM"] + 1)
                mr = cs.new(b["MR"] + 0)
                dv1 = cs.new(b["D1"] + 1)
                dv2g = cs.new(b["D2"] + 2) if double else None
                stg = cs.new(b["ST"] + 32 + 16 - 32 + 16) if double else None
                if double:
                    sc.reg_mov(stg.reg, b["ST"] + 16)
                ap3 = cs.new(b["AP"] + apc) if hs else None
                sr2 = cs.new(b2["SR"] + 0) if hs else None
                with kloop(sc, nch):
                    for e in (0,):
                        W(sc, LD.h, ld, 16)
                        if hs:
                            W(sc, AP_.h, ap3, apc)
                            W(sc, SR.h, sr2, 16)
                        for r in range(R):
                            src_t = xt[e] if r == 0 else u1t
                            if r == 1:
                                W(sc, DV2.h, dv2r, R)
                            last = None
                            for dg in range(8):
                                last = sc.activation(sqt[:, dg, :],
                                                     src_t[:, dg, :],
                                                     AF.Square)
                            SQ.inc(last, 1)
                            W(sc, SM.h, sm, 1)
                            W(sc, MR.h, mr, 1)
                            sc.activation(rmm[:], PS[0][0:1, :], AF.Copy,
                                          scale=np.float32(1.0 / D))
                            sc.activation(rma[:], PS[1][0:1, :], AF.Copy,
                                          scale=np.float32(1.0 / D))
                            EV1.inc(sc.activation(rmb[:], rmm[:],
                                                  AF.Square), 1)
                            W(sc, DV1.h, dv1, 1)
                            sc.activation(rma[:], rmb[:], AF.Ln,
                                          bias=np.float32(EPS))
                            EV2.inc(sc.activation(rmr[:], rma[:], AF.Exp,
                                                  scale=np.float32(-0.5)), 1)
                        if double:
                            W(sc, DV2.h, dv2g, 2)
                            W(sc, ST_.h, stg, 16)
                            last = None
                            for dg in range(8):
                                last = sc.activation(
                                    xnt[e][:, dg, :], nt[:, dg, :],
                                    AF.Identity,
                                    scale=gam_sb[:, dg:dg + 1],
                                    bias=bet_sb[:, dg:dg + 1])
                            AP_.inc(last, 1)
                cs.free()

            @blk.vector
            def _(v):
                if not double:
                    AP2seed = 0
                cs = Cs(v)
                stv = cs.new(b["ST"] + 16) if not double else None
                ev1 = cs.new(b["E1"] + 1)
                ev2 = cs.new(b["E2"] + 1)
                bcm = cs.new(b["BC"] + 0)
                bc = cs.new(b["BC"] + 1)
                sr = cs.new(b2["SR"] + 0) if hs else None
                with kloop(v, nch):
                    for e in (0,):
                        if not double:
                            W(v, ST_.h, stv, 16)
                        for r in range(R):
                            src_t = xt[e] if r == 0 else u1t
                            W(v, EV1.h, ev1, 1)
                            DV1.inc(v.tensor_sub(rmb[:], rma[:], rmb[:]), 1)
                            W(v, EV2.h, ev2, 1)
                            W(v, BC.h, bcm, 1)
                            MR.inc(v.tensor_mul(rmc[:], rmm[:], rmr[:]), 1)
                            W(v, BC.h, bc, 1)
                            dst_t = (u1t if r == 0 else nt) if double \
                                else None
                            last = None
                            lmul = None
                            for dg in range(8):
                                lmul = v.tensor_mul(sqt[:, dg, :],
                                                    src_t[:, dg, :],
                                                    PS[2][:, 0:TCH])
                                tgt = (dst_t[:, dg, :] if dst_t is not None
                                       else xnt[e][:, dg, :])
                                last = v.tensor_sub(tgt, sqt[:, dg, :],
                                                    PS[3][:, 0:TCH])
                            if double and r == 0:
                                AP_.inc(lmul, 1)
                            DV2.inc(last, 1)
                        if hs:
                            W(v, SR.h, sr, 16)
                            last = None
                            for dg in range(8):
                                last = v.tensor_add(sqt[:, dg, :],
                                                    xt[e][:, dg, :],
                                                    u1t[:, dg, :])
                            AP_.inc(last, 1)
                cs.free()

            @blk.tensor
            def _(t):
                cs = Cs(t)
                sq = cs.new(b["SQ"] + 1)
                ev1f = cs.new(b["E1"] + 0)
                dv2f = cs.new(b["D2"] + 0)
                mrt = cs.new(b["MR"] + 1)
                with kloop(t, nch):
                    for e in (0,):
                        for r in range(R):
                            src_t = xt[e] if r == 0 else u1t
                            W(t, SQ.h, sq, 1)
                            W(t, EV1.h, ev1f, 1)
                            for dg in range(8):
                                t.matmul(PS[0][0:1, :], ones_c[:],
                                         src_t[:, dg, :],
                                         start=(dg == 0), stop=(dg == 7))
                            last = None
                            for dg in range(8):
                                last = t.matmul(PS[1][0:1, :], ones_c[:],
                                                sqt[:, dg, :],
                                                start=(dg == 0),
                                                stop=(dg == 7))
                            SM.inc(last, 1)
                            W(t, DV2.h, dv2f, 1)
                            W(t, MR.h, mrt, 1)
                            t.matmul(PS[2][:, 0:TCH], ones_r[:], rmr[:],
                                     start=True, stop=True)
                            BC.inc(t.matmul(PS[3][:, 0:TCH], ones_r[:],
                                            rmc[:], start=True, stop=True),
                                   1)
                cs.free()
        _pe()

    def P_stream(tag, inp_d, KG, wsrc, OG, out_d, T, actf,
                 scale_sb=None, bias_sb=None, out_dt=BF16, dve="none",
                 res_d=None, store_off=0, res_off=0, wcol_off=0):
        _pb()
        TBW = 1024
        NSUB = TBW // 512
        nblk = T // TBW
        LD, WL, WLB, M1, EVa, DVa, ST_ = (sem(f"{tag}{x}") for x in
                                          ("LD", "WL", "WB", "M1", "EV",
                                           "DV", "ST"))
        has_s = scale_sb is not None
        has_b = bias_sb is not None
        has_r = dve == "res"
        need_dve = dve != "none"
        b = dict(LD=LD.total, WL=WL.total, WB=WLB.total, M1=M1.total,
                 EV=EVa.total, DV=DVa.total, ST=ST_.total)
        inb = sbuf(f"{tag}in", [128, KG, TBW], BF16)
        mkb = sbuf(f"{tag}mk", [128, TBW], F32) if dve == "mask" else None
        resb = sbuf(f"{tag}res", [128, 8, TBW], F32) if has_r else None
        wt = [sbuf(f"{tag}w{i}", [128, KG, 128], BF16) for i in range(2)]
        ft = sbuf(f"{tag}ft", [128, TBW], F32) if need_dve else None
        ob = sbuf(f"{tag}ob", [128, OG, TBW], out_dt)
        inv = inp_d.ap().rearrange("g p t -> p g t")
        outv = out_d.ap().rearrange("g p t -> p g t")
        resv = res_d.ap().rearrange("g p t -> p g t") if has_r else None
        wv = wsrc if isinstance(wsrc, bass.AP) else \
            wsrc.ap().rearrange("(kg p) o -> p kg o", p=128)
        if wcol_off:
            wv = wv[:, :, wcol_off:]
        nld = 16 * (1 + (dve == "mask") + has_r)

        with nc.Block() as blk:
            @blk.gpsimd
            def _(g):
                cs = Cs(g)
                ldg = cs.new(b["M1"] + 0)
                dvg = cs.new(b["DV"] + 2) if need_dve else None
                in_of = cs.new(int(inv[:, :, 0:TBW].offset))
                mk_of = cs.new(int(maskv.ap()[0:1, 0:TBW].offset)) \
                    if dve == "mask" else None
                r_of = cs.new(int(resv[:, :, res_off:res_off + TBW].offset)) \
                    if has_r else None
                wlga = cs.new(b["EV"] + NSUB)
                with kloop(g, nblk):
                    W(g, M1.h, ldg, OG)
                    LD.inc(g.dma_start(out=inb[:],
                                       in_=dyn(inv[:, :, 0:TBW], in_of)), 16)
                    in_of.add(TBW)
                    if dve == "mask":
                        W(g, DVa.h, dvg, OG)
                        LD.inc(g.dma_start(
                            out=mkb[:],
                            in_=dyn(maskv.ap()[0:1, 0:TBW], mk_of)
                            .partition_broadcast(128).squeeze(1)), 16)
                        mk_of.add(TBW)
                    if has_r:
                        W(g, DVa.h, dvg, OG)
                        LD.inc(g.dma_start(
                            out=resb[:],
                            in_=dyn(resv[:, :, 0:TBW], r_of)), 16)
                        r_of.add(TBW)
                    for o in range(0, OG, 2):
                        W(g, EVa.h, wlga, 2 * NSUB)
                        WL.inc(g.dma_start(
                            out=wt[0][:],
                            in_=wv[:, :, o * 128:(o + 1) * 128]), 16)
                cs.free()

            @blk.sync
            def _(sy):
                cs = Cs(sy)
                wlgb = cs.new(b["EV"] + 2 * NSUB)
                with kloop(sy, nblk):
                    for o in range(1, OG, 2):
                        W(sy, EVa.h, wlgb, 2 * NSUB)
                        WLB.inc(sy.dma_start(
                            out=wt[1][:],
                            in_=wv[:, :, o * 128:(o + 1) * 128]), 16)
                cs.free()

            @blk.tensor
            def _(t):
                cs = Cs(t)
                ld = cs.new(b["LD"] + nld)
                wl = cs.new(b["WL"] + 16)
                wlb2 = cs.new(b["WB"] + 16)
                evb = cs.new(b["EV"] + NSUB)
                with kloop(t, nblk):
                    W(t, LD.h, ld, nld)
                    with kloop(t, OG // 2):
                        for e in range(2):
                            W(t, (WL if e == 0 else WLB).h,
                              wl if e == 0 else wlb2, 16)
                            W(t, EVa.h, evb, NSUB)
                            last = None
                            for kh in range(KG):
                                for s_ in range(NSUB):
                                    last = t.matmul(
                                        PS[e * NSUB + s_][:],
                                        wt[e][:, kh, :],
                                        inb[:, kh,
                                            s_ * 512:(s_ + 1) * 512],
                                        start=(kh == 0),
                                        stop=(kh == KG - 1))
                            M1.inc(last, 1)
                cs.free()

            @blk.scalar
            def _(sc):
                EVa.seed(sc, 2 * NSUB)
                ST_.seed(sc, 16)
                cs = Cs(sc)
                m1 = cs.new(b["M1"] + 1)
                stf = cs.new(b["ST"] + 16)
                dvo = cs.new(b["DV"] + 2) if need_dve else None
                dvs = cs.new(b["DV"] + 2 + OG) if need_dve else None
                o_of = cs.new(int(outv[:, 0:1,
                                       store_off:store_off + TBW].offset))
                with kloop(sc, nblk):
                    W(sc, ST_.h, stf, 16)
                    for o in range(OG):
                        e = o % 2
                        W(sc, M1.h, m1, 1)
                        if need_dve:
                            W(sc, DVa.h, dvo, 1)
                        kw = {}
                        if has_s:
                            kw["scale"] = scale_sb[:, o:o + 1]
                        if has_b:
                            kw["bias"] = bias_sb[:, o:o + 1]
                        last = None
                        for s_ in range(NSUB):
                            dst = (ft[:, s_ * 512:(s_ + 1) * 512]
                                   if need_dve else
                                   ob[:, o, s_ * 512:(s_ + 1) * 512])
                            last = sc.activation(dst, PS[e * NSUB + s_][:],
                                                 actf, **kw)
                        EVa.inc(last, NSUB)
                    if need_dve:
                        W(sc, DVa.h, dvs, OG)
                    ST_.inc(sc.dma_start(
                        out=dyn(outv[:, 0:OG, 0:TBW], o_of),
                        in_=ob.ap()), 16)
                    o_of.add(TBW)
                cs.free()
                sc.wait_ge(ST_.h, ST_.total)

            if need_dve:
                @blk.vector
                def _(v):
                    DVa.seed(v, 2)
                    cs = Cs(v)
                    ev = cs.new(b["EV"] + 3 * NSUB)
                    with kloop(v, nblk):
                        for o in range(OG):
                            W(v, EVa.h, ev, NSUB)
                            if has_r:
                                DVa.inc(v.tensor_add(ob[:, o, :], ft[:],
                                                     resb[:, o % 8, :]), 1)
                            else:
                                DVa.inc(v.tensor_mul(ob[:, o, :], ft[:],
                                                     mkb[:]), 1)
                    cs.free()
        _pe()


    def P_ffn_in(tag, xnsrc, aout, w1v_, w2v_, T):
        _pb()
        TBW = 1024
        NSUB = 2
        OG = 32
        nblk = T // TBW
        LD, WL, WL2, M1, M2, EVa, EVb, GL, ST_ = (
            sem(f"{tag}{x}") for x in
            ("LD", "WL", "W2", "M1", "M2", "Ea", "Eb", "GL", "ST"))
        b = dict(LD=LD.total, WL=WL.total, W2=WL2.total, M1=M1.total,
                 M2=M2.total, Ea=EVa.total, Eb=EVb.total, GL=GL.total,
                 ST=ST_.total)
        inb = sbuf(f"{tag}in", [128, 8, TBW], BF16)
        wt = [sbuf(f"{tag}w{i}", [128, 8, 128], BF16) for i in range(4)]
        p1t = sbuf(f"{tag}p1", [128, TBW], F32)
        p2t = sbuf(f"{tag}p2", [128, TBW], F32)
        ab = sbuf(f"{tag}ab", [128, OG, TBW], BF16)
        xnv = xnsrc.ap().rearrange("g p t -> p g t")
        av = aout.ap().rearrange("g p t -> p g t")

        with nc.Block() as blk:
            @blk.gpsimd
            def _(g):
                cs = Cs(g)
                ldg = cs.new(b["M2"] + 0)
                in_of = cs.new(int(xnv[:, :, 0:TBW].offset))
                wla = cs.new(b["Ea"] + 0)
                with kloop(g, nblk):
                    W(g, M2.h, ldg, OG)
                    LD.inc(g.dma_start(out=inb[:],
                                       in_=dyn(xnv[:, :, 0:TBW], in_of)),
                           16)
                    in_of.add(TBW)
                    for o in range(OG):
                        e = o % 2
                        W(g, EVa.h, wla, NSUB)
                        WL.inc(g.dma_start(
                            out=wt[e][:],
                            in_=w1v_[:, :, o * 128:(o + 1) * 128]), 16)
                cs.free()

            @blk.sync
            def _(sy):
                cs = Cs(sy)
                wlb = cs.new(b["Eb"] + 0)
                with kloop(sy, nblk):
                    for o in range(OG):
                        e = o % 2
                        W(sy, EVb.h, wlb, NSUB)
                        WL2.inc(sy.dma_start(
                            out=wt[2 + e][:],
                            in_=w2v_[:, :, o * 128:(o + 1) * 128]), 16)
                cs.free()

            @blk.tensor
            def _(t):
                cs = Cs(t)
                ld = cs.new(b["LD"] + 16)
                wl = cs.new(b["WL"] + 16)
                wl2 = cs.new(b["W2"] + 16)
                ea = cs.new(b["Ea"] + NSUB)
                eb = cs.new(b["Eb"] + NSUB)
                with kloop(t, nblk):
                    W(t, LD.h, ld, 16)
                    with kloop(t, OG // 2):
                        for e in range(2):
                            W(t, WL.h, wl, 16)
                            W(t, EVa.h, ea, NSUB)
                            last = None
                            for kh in range(8):
                                for s_ in range(NSUB):
                                    last = t.matmul(
                                        PS[s_][:], wt[e][:, kh, :],
                                        inb[:, kh,
                                            s_ * 512:(s_ + 1) * 512],
                                        start=(kh == 0), stop=(kh == 7))
                            M1.inc(last, 1)
                            W(t, WL2.h, wl2, 16)
                            W(t, EVb.h, eb, NSUB)
                            last = None
                            for kh in range(8):
                                for s_ in range(NSUB):
                                    last = t.matmul(
                                        PS[4 + s_][:], wt[2 + e][:, kh, :],
                                        inb[:, kh,
                                            s_ * 512:(s_ + 1) * 512],
                                        start=(kh == 0), stop=(kh == 7))
                            M2.inc(last, 1)
                cs.free()

            @blk.scalar
            def _(sc):
                EVa.seed(sc, NSUB)
                EVb.seed(sc, NSUB)
                ST_.seed(sc, 16)
                cs = Cs(sc)
                m1 = cs.new(b["M1"] + 1)
                m2 = cs.new(b["M2"] + 1)
                gl = cs.new(b["GL"] + 1)
                gls = cs.new(b["GL"] + 1 + OG)
                a_of = cs.new(int(av[:, 0:OG, 0:TBW].offset))
                stf = cs.new(b["ST"] + 16)
                with kloop(sc, nblk):
                    W(sc, ST_.h, stf, 16)
                    for o in range(OG):
                        e = o % 2
                        W(sc, M1.h, m1, 1)
                        W(sc, GL.h, gl, 1)
                        last = None
                        for s_ in range(NSUB):
                            last = sc.activation(
                                p1t[:, s_ * 512:(s_ + 1) * 512],
                                PS[s_][:], AF.Silu,
                                scale=s1_sb[:, o:o + 1])
                        EVa.inc(last, NSUB)
                        W(sc, M2.h, m2, 1)
                        last = None
                        for s_ in range(NSUB):
                            last = sc.activation(
                                p2t[:, s_ * 512:(s_ + 1) * 512],
                                PS[4 + s_][:], AF.Copy,
                                scale=s2_sb[:, o:o + 1])
                        EVb.inc(last, NSUB)
                    W(sc, GL.h, gls, OG)
                    ST_.inc(sc.dma_start(
                        out=dyn(av[:, 0:OG, 0:TBW], a_of), in_=ab.ap()), 16)
                    a_of.add(TBW)
                cs.free()
                sc.wait_ge(ST_.h, ST_.total)

            @blk.vector
            def _(v):
                GL.seed(v, 1)
                cs = Cs(v)
                ebv = cs.new(b["Eb"] + 2 * NSUB)
                with kloop(v, nblk):
                    for o in range(OG):
                        W(v, EVb.h, ebv, NSUB)
                        GL.inc(v.tensor_mul(ab[:, o, :], p1t[:], p2t[:]), 1)
                cs.free()
        _pe()

    def P_conv():
        _pb()
        DG, XL, CM, EVc, CC_, ST_ = (sem(f"cv{x}") for x in
                                     ("DG", "XL", "CM", "EV", "CC", "ST"))
        b = dict(DG=DG.total, XL=XL.total, CM=CM.total, EV=EVc.total,
                 CC=CC_.total, ST=ST_.total)
        dgt = sbuf("cvd", [128, 2, KC, 128], BF16)
        xit = sbuf("cvx", [128, 2, PAD + TL], BF16)
        xct = sbuf("cvc", [128, TL], F32)
        xbt = sbuf("cvbt", [128, TL], BF16)
        cwbst = sbuf("cvcw", [128, 2, KC + 1], F32)
        cwbtm = convwbv.ap()[:, 0:1, :]
        xitm = xin_d.ap()[0:1].squeeze(0)
        xctm = xc_d.ap()[0:1].squeeze(0)
        xbtm = xcb_d.ap()[0:1].squeeze(0)
        xcc_eg0 = bass.AP(dmxc_d, 16 * 128 * TCH,
                          [[TCH, 128], [2 * 16 * 128 * TCH, NCH], [1, TCH]])
        xcv2 = xct.ap().rearrange("p (c t) -> p c t", t=TCH)

        with nc.Block() as blk:
            @blk.sync
            def _(sy):
                cs = Cs(sy)
                cmf = cs.new(b["CM"] + 0)
                dgf = cs.new(b["DG"] + 0)
                evs = cs.new(b["EV"] + 4)
                xi_of = cs.new(0)
                cw_of = cs.new(int(cwbtm.offset))
                with kloop(sy, 8):
                    for e in range(2):
                        W(sy, CM.h, cmf, 16)
                        W(sy, DG.h, dgf, 1)
                        XL.inc(sy.dma_start(out=xit[:, e, :],
                                            in_=dyn(xitm, xi_of)), 16)
                        xi_of.add(128 * (PAD + TL))
                        W(sy, EVc.h, evs, 16)
                        XL.inc(sy.dma_start(
                            out=cwbst.ap()[:, e:e + 1, :],
                            in_=dyn(cwbtm, cw_of)), 16)
                        cw_of.add(KC + 1)
                cs.free()

            @blk.gpsimd
            def _(g):
                ST_.seed(g, 48)
                cs = Cs(g)
                ccr = cs.new(b["CC"] + 2)
                xc_of = cs.new(0)
                xb_of = cs.new(0)
                xcc_of = cs.new(int(xcc_eg0.offset))
                with kloop(g, 16):
                    W(g, CC_.h, ccr, 1)
                    ST_.inc(g.dma_start(out=dyn(xctm, xc_of),
                                        in_=xct[:]), 16)
                    xc_of.add(128 * TL)
                    ST_.inc(g.dma_start(out=dyn(xbtm, xb_of),
                                        in_=xbt[:]), 16)
                    xb_of.add(128 * TL)
                    ST_.inc(g.dma_start(out=dyn(xcc_eg0, xcc_of),
                                        in_=xcv2[:]), 16)
                    xcc_of.add(128 * TCH)
                cs.free()
                g.wait_ge(ST_.h, ST_.total)

            @blk.vector
            def _(v):
                CC_.seed(v, 1)
                DG.seed(v, 1)
                cs = Cs(v)
                cmd = cs.new(b["CM"] + 0)
                xlv = cs.new(b["XL"] + 32)
                evr = cs.new(b["EV"] + 36)
                stv = cs.new(b["ST"] + 48)
                with kloop(v, 8):
                    for e in range(2):
                        W(v, CM.h, cmd, 16)
                        W(v, XL.h, xlv, 32)
                        last = None
                        for kk in range(KC):
                            last = v.tensor_scalar(dgt[:, e, kk, :],
                                                   eye_sb[:],
                                                   cwbst[:, e, kk:kk + 1],
                                                   None, ALU.mult)
                        DG.inc(last, 1)
                        W(v, EVc.h, evr, 16)
                        W(v, ST_.h, stv, 48)
                        CC_.inc(v.tensor_copy(xbt[:], xct[:]), 1)
                cs.free()

            @blk.tensor
            def _(t):
                CM.seed(t, 16)
                cs = Cs(t)
                dg = cs.new(b["DG"] + 2)
                xl = cs.new(b["XL"] + 32)
                ev = cs.new(b["EV"] + 17)
                with kloop(t, 8):
                    for e in range(2):
                        W(t, DG.h, dg, 1)
                        W(t, XL.h, xl, 32)
                        for ch in range(NCH):
                            W(t, EVc.h, ev, 1)
                            last = None
                            for kk in range(KC):
                                last = t.matmul(
                                    PS[ch % 4][:], dgt[:, e, kk, :],
                                    xit[:, e,
                                        ch * TCH + kk:ch * TCH + kk + TCH],
                                    start=(kk == 0), stop=(kk == KC - 1))
                            CM.inc(last, 1)
                cs.free()

            @blk.scalar
            def _(sc):
                EVc.seed(sc, 20)
                cs = Cs(sc)
                sts = cs.new(b["ST"] + 48)
                cm = cs.new(b["CM"] + 16 + 1)
                with kloop(sc, 8):
                    for e in range(2):
                        W(sc, ST_.h, sts, 48)
                        for ch in range(NCH):
                            W(sc, CM.h, cm, 1)
                            EVc.inc(sc.activation(
                                xct[:, ch * TCH:(ch + 1) * TCH],
                                PS[ch % 4][:], AF.Silu,
                                bias=cwbst[:, e, KC:KC + 1]), 1)
                cs.free()
        _pe()

    def P_wxdt():
        _pb()
        LD, MX, EX, MD, ED_, DV, ST_ = (sem(f"wx{x}") for x in
                                        ("LD", "MX", "EX", "MD", "ED", "DV",
                                         "ST"))
        b = dict(LD=LD.total, MX=MX.total, EX=EX.total, MD=MD.total,
                 ED=ED_.total, DV=DV.total, ST=ST_.total)
        wdt_sb = sbuf("wdt_sb", [RK, 16, 128], BF16)
        wx_sb = sbuf("wx_sb", [128, 16, 128], BF16)
        xbt = sbuf("wxxb", [128, 16, TCH], BF16)
        mkt = sbuf("wxmk", [128, TCH], F32)
        dtl = sbuf("wxdt1", [RK, TCH], BF16)
        bct = sbuf("wxbc", [64, TCH], F32)
        t2b = sbuf("wxt2", [128, 16, TCH], F32)
        t4b = sbuf("wxt4", [128, 16, TCH], F32)
        dmt = sbuf("wxdm", [128, 16, TCH], F32)
        xbv = xcb_d.ap().rearrange("g p t -> p g t")
        mktm = maskv.ap()[0:1, 0:TCH]
        dm2tm = bass.AP(dmxc_d, 0, [[TCH, 128], [128 * TCH, 16], [1, TCH]])
        bctm = bc_d.ap()[:, 0:TCH]

        with nc.Block() as blk:
            @blk.gpsimd
            def _(g):
                LD.inc(g.dma_start(
                    out=wdt_sb[:],
                    in_=wdtT.ap().rearrange("r (og p) -> r og p", p=128)),
                    16)
                LD.inc(g.dma_start(
                    out=wx_sb[:],
                    in_=wxT.ap().rearrange("(eg p) o -> p eg o", p=128)), 16)

            @blk.sync
            def _(sy):
                ST_.seed(sy, 32)
                cs = Cs(sy)
                dvf = cs.new(b["DV"] + 32)
                dvr = cs.new(b["DV"] + 48)
                exr = cs.new(b["EX"] + 2 + 2)
                xb_of = cs.new(int(xbv[:, :, 0:TCH].offset))
                mk_of = cs.new(int(mktm.offset))
                dm_of = cs.new(0)
                bo_of = cs.new(int(bctm.offset))
                with kloop(sy, NCH):
                    W(sy, DV.h, dvf, 16)
                    LD.inc(sy.dma_start(out=xbt[:],
                                        in_=dyn(xbv[:, :, 0:TCH], xb_of)),
                           16)
                    xb_of.add(TCH)
                    LD.inc(sy.dma_start(
                        out=mkt[:],
                        in_=dyn(mktm, mk_of)
                        .partition_broadcast(128).squeeze(1)), 16)
                    mk_of.add(TCH)
                    W(sy, DV.h, dvr, 16)
                    ST_.inc(sy.dma_start(out=dyn(dm2tm, dm_of),
                                         in_=dmt[:]), 16)
                    dm_of.add(2 * 16 * 128 * TCH)
                    W(sy, EX.h, exr, 2)
                    ST_.inc(sy.dma_start(out=dyn(bctm, bo_of),
                                         in_=bct[0:32, :]), 16)
                    bo_of.add(TCH)
                cs.free()
                sy.wait_ge(ST_.h, ST_.total)

            @blk.tensor
            def _(t):
                cs = Cs(t)
                ld = cs.new(b["LD"] + 64)
                exf = cs.new(b["EX"] + 2)
                exd = cs.new(b["EX"] + 4)
                ed = cs.new(b["ED"] + 1)
                with kloop(t, NCH):
                    W(t, LD.h, ld, 32)
                    W(t, EX.h, exf, 2)
                    last = None
                    for kg in range(16):
                        last = t.matmul(PS[4][:], wx_sb[:, kg, :],
                                        xbt[:, kg, :], start=(kg == 0),
                                        stop=(kg == 15))
                    MX.inc(last, 1)
                    W(t, EX.h, exd, 2)
                    for o in range(16):
                        W(t, ED_.h, ed, 1)
                        MD.inc(t.matmul(PS[5 + o % 2][:], wdt_sb[:, o, :],
                                        dtl[:], start=True, stop=True), 1)
                cs.free()

            @blk.scalar
            def _(sc):
                EX.seed(sc, 2)
                ED_.seed(sc, 2)
                cs = Cs(sc)
                mx = cs.new(b["MX"] + 1)
                stb = cs.new(b["ST"] + 32)
                dvt = cs.new(b["DV"] + 32)
                md = cs.new(b["MD"] + 1)
                with kloop(sc, NCH):
                    W(sc, MX.h, mx, 1)
                    W(sc, ST_.h, stb, 32)
                    sc.activation(dtl[:], PS[4][0:64, :], AF.Copy)
                    EX.inc(sc.activation(bct[:], PS[4][64:128, :],
                                         AF.Copy), 2)
                    W(sc, DV.h, dvt, 16)
                    for o in range(16):
                        W(sc, MD.h, md, 1)
                        sc.activation(t2b[:, o, :], PS[5 + o % 2][:],
                                      AF.Abs, bias=bdt_sb[:, o:o + 1])
                        sc.activation(t2b[:, o, :], t2b[:, o, :], AF.Exp,
                                      scale=np.float32(-1.0))
                        sc.activation(t2b[:, o, :], t2b[:, o, :], AF.Ln,
                                      bias=np.float32(1.0))
                        ED_.inc(sc.activation(t4b[:, o, :],
                                              PS[5 + o % 2][:], AF.Relu,
                                              bias=bdt_sb[:, o:o + 1]), 1)
                cs.free()

            @blk.vector
            def _(v):
                DV.seed(v, 32)
                cs = Cs(v)
                stv = cs.new(b["ST"] + 32)
                edv = cs.new(b["ED"] + 2 + 1)
                with kloop(v, NCH):
                    W(v, ST_.h, stv, 32)
                    for o in range(16):
                        W(v, ED_.h, edv, 1)
                        v.tensor_add(dmt[:, o, :], t2b[:, o, :],
                                     t4b[:, o, :])
                        DV.inc(v.tensor_mul(dmt[:, o, :], dmt[:, o, :],
                                            mkt[:]), 1)
                cs.free()
        _pe()

    def P_scan(part):
        _pb()
        tag = f"sn{part}"
        LD, BR, DA, VF, YD, ST_, VN = (sem(f"{tag}{x}") for x in
                                       ("LD", "BR", "DA", "VF", "YD", "ST",
                                        "VN"))
        b = dict(LD=LD.total, BR=BR.total, DA=DA.total, VF=VF.total,
                 YD=YD.total, ST=ST_.total, VN=VN.total)
        with_y = part == "b"
        dxb = sbuf(f"{tag}dxb", [128, 2, 16, TCH], F32)
        ust = sbuf(f"{tag}u", [128, TCH], F32)
        dab = sbuf(f"{tag}da", [128, 2, TCH], F32)
        dbx = sbuf(f"{tag}db", [128, TCH], F32)
        hti = sbuf(f"{tag}h", [128, TCH], F32)
        brep = sbuf(f"{tag}br", [128, NS, TCH], F32)
        crep = sbuf(f"{tag}cr", [128, NS, TCH], F32) if with_y else None
        ybig = sbuf(f"{tag}y", [128, 16, TCH], F32) if with_y else None
        coff = PO if with_y else 0
        ch0 = NPO if with_y else 0
        nbr = 32 if with_y else 16
        dmctm = bass.AP(dmxc_d, ch0 * 2 * 16 * 128 * TCH,
                        [[TCH, 128], [16 * 128 * TCH, 2],
                         [128 * TCH, 16], [1, TCH]])
        yctm = bass.AP(y_d, 0, [[TCH, 128], [128 * TCH, 16], [1, TCH]])
        btm = bc_d.ap()[0:16, coff:coff + TCH]
        ctm = bc_d.ap()[16:32, coff:coff + TCH]

        with nc.Block() as blk:
            @blk.sync
            def _(sy):
                ST_.seed(sy, 16)
                cs = Cs(sy)
                vfb = cs.new(b["VF"] + 16)
                dm_of = cs.new(int(dmctm.offset))
                yd_ = cs.new(b["YD"] + 16) if with_y else None
                y_of = cs.new(0) if with_y else None
                with kloop(sy, 8):
                    W(sy, VF.h, vfb, 16)
                    LD.inc(sy.dma_start(out=dxb[:], in_=dyn(dmctm, dm_of)),
                           16)
                    dm_of.add(2 * 16 * 128 * TCH)
                    if with_y:
                        W(sy, YD.h, yd_, 16)
                        ST_.inc(sy.dma_start(out=dyn(yctm, y_of),
                                             in_=ybig[:]), 16)
                        y_of.add(16 * 128 * TCH)
                cs.free()
                if with_y:
                    sy.wait_ge(ST_.h, ST_.total)

            @blk.scalar
            def _(sc):
                cs = Cs(sc)
                ld = cs.new(b["LD"] + 16)
                vnn = cs.new(b["VN"] + 1)
                vfbr = cs.new(b["VF"] + 16)
                br_of = cs.new(int(btm.offset))
                cr_of = cs.new(int(ctm.offset)) if with_y else None
                with kloop(sc, 8):
                    W(sc, LD.h, ld, 16)
                    W(sc, VF.h, vfbr, 16)
                    BR.inc(sc.dma_start(
                        out=brep[:],
                        in_=dyn(btm, br_of).partition_broadcast(128)), 16)
                    br_of.add(TCH)
                    if with_y:
                        BR.inc(sc.dma_start(
                            out=crep[:],
                            in_=dyn(ctm, cr_of)
                            .partition_broadcast(128)), 16)
                        cr_of.add(TCH)
                    for eg in range(16):
                        for nn in range(NS):
                            W(sc, VN.h, vnn, 1)
                            DA.inc(sc.activation(
                                dab[:, nn % 2, :], dxb[:, 0, eg, :], AF.Exp,
                                scale=A_sb[:, eg * NS + nn:
                                           eg * NS + nn + 1]), 1)
                cs.free()

            @blk.vector
            def _(v):
                VF.seed(v, 16)
                VN.seed(v, 2)
                cs = Cs(v)
                br = cs.new(b["BR"] + nbr)
                da = cs.new(b["DA"] + 1)
                stv = cs.new(b["ST"] + 16) if with_y else None
                with kloop(v, 8):
                    W(v, BR.h, br, nbr)
                    if with_y:
                        W(v, ST_.h, stv, 16)
                    for eg in range(16):
                        v.tensor_mul(ust[:], dxb[:, 0, eg, :],
                                     dxb[:, 1, eg, :])
                        last_c = None
                        last_y = None
                        for nn in range(NS):
                            cl = eg * NS + nn
                            W(v, DA.h, da, 1)
                            v.tensor_mul(dbx[:], ust[:], brep[:, nn, :])
                            sci = v.tensor_tensor_scan(
                                hti[:], dab[:, nn % 2, :], dbx[:],
                                carry[:, cl:cl + 1], ALU.mult, ALU.add)
                            VN.inc(sci, 1)
                            last_c = v.tensor_copy(carry[:, cl:cl + 1],
                                                   hti[:, TCH - 1:TCH])
                            if with_y:
                                cm_ = v.tensor_mul(hti[:], hti[:],
                                                   crep[:, nn, :])
                                if nn == NS - 1:
                                    VF.inc(cm_, 1)
                                if nn == 0:
                                    last_y = v.tensor_copy(
                                        ybig[:, eg, :], hti[:])
                                else:
                                    last_y = v.tensor_add(
                                        ybig[:, eg, :], ybig[:, eg, :],
                                        hti[:])
                        if with_y:
                            YD.inc(last_y, 1)
                        else:
                            VF.inc(last_c, 1)
                cs.free()
        _pe()

    def P_gate():
        _pb()
        LD, SZ, DV, ST_ = (sem(f"gt{x}") for x in ("LD", "SZ", "DV", "ST"))
        b = dict(LD=LD.total, SZ=SZ.total, DV=DV.total, ST=ST_.total)
        yt = sbuf("gty", [128, 16, TCH], F32)
        xct = sbuf("gtxc", [128, 16, TCH], F32)
        zt = sbuf("gtz", [128, 16, TCH], F32)
        szt = sbuf("gtsz", [128, 16, TCH], F32)
        ygt = sbuf("gtyg", [128, 16, TCH], BF16)
        yctm = bass.AP(y_d, 0, [[TCH, 128], [128 * TCH, 16], [1, TCH]])
        xcv = xc_d.ap().rearrange("g p t -> p g t")
        zv = z_d.ap().rearrange("g p t -> p g t")
        ygv = yg_d.ap().rearrange("g p t -> p g t")

        with nc.Block() as blk:
            @blk.sync
            def _(sy):
                cs = Cs(sy)
                dvy = cs.new(b["DV"] + 32)
                y_of = cs.new(0)
                xc_of = cs.new(int(xcv[:, :, PO:PO + TCH].offset))
                with kloop(sy, NPO):
                    W(sy, DV.h, dvy, 32)
                    LD.inc(sy.dma_start(out=yt[:], in_=dyn(yctm, y_of)), 16)
                    y_of.add(16 * 128 * TCH)
                    LD.inc(sy.dma_start(
                        out=xct[:],
                        in_=dyn(xcv[:, :, 0:TCH], xc_of)), 16)
                    xc_of.add(TCH)
                cs.free()

            @blk.scalar
            def _(sc):
                ST_.seed(sc, 16)
                cs = Cs(sc)
                dvf = cs.new(b["DV"] + 32)
                dvr = cs.new(b["DV"] + 64)
                z_of = cs.new(int(zv[:, :, PO:PO + TCH].offset))
                yg_of = cs.new(int(ygv[:, :, 0:TCH].offset))
                ld = cs.new(b["LD"] + 48)
                dvs = cs.new(b["DV"] + 32)
                with kloop(sc, NPO):
                    W(sc, DV.h, dvf, 32)
                    LD.inc(sc.dma_start(out=zt[:],
                                        in_=dyn(zv[:, :, 0:TCH], z_of)), 16)
                    z_of.add(TCH)
                    W(sc, LD.h, ld, 48)
                    W(sc, DV.h, dvs, 32)
                    last = None
                    for eg in range(16):
                        last = sc.activation(szt[:, eg, :], zt[:, eg, :],
                                             AF.Silu)
                    SZ.inc(last, 1)
                    W(sc, DV.h, dvr, 32)
                    ST_.inc(sc.dma_start(out=dyn(ygv[:, :, 0:TCH], yg_of),
                                         in_=ygt[:]), 16)
                    yg_of.add(TCH)
                cs.free()
                sc.wait_ge(ST_.h, ST_.total)

            @blk.vector
            def _(v):
                DV.seed(v, 32)
                cs = Cs(v)
                sz = cs.new(b["SZ"] + 1)
                stv = cs.new(b["ST"] + 16)
                with kloop(v, NPO):
                    W(v, SZ.h, sz, 1)
                    W(v, ST_.h, stv, 16)
                    for eg in range(16):
                        DV.inc(v.scalar_tensor_tensor(
                            yt[:, eg, :], xct[:, eg, :],
                            dpar_sb[:, eg:eg + 1], yt[:, eg, :],
                            ALU.mult, ALU.add), 1)
                        DV.inc(v.tensor_mul(ygt[:, eg, :], yt[:, eg, :],
                                            szt[:, eg, :]), 1)
                cs.free()
        _pe()

    P_ln("l1", x_f, xn1_d, TL, False)
    P_ffn_in("fa", xn1_d, a_d, wview("w1b"), wview("w2b"), TL)
    P_stream("fb", a_d, 32, wview("h1b"), 32, bb_d, TL, AF.Gelu,
             scale_sb=sh1_sb)
    P_stream("fc", bb_d, 32, wview("h2b"), 32, cc_d, TL, AF.Gelu,
             scale_sb=sh2_sb)
    P_stream("fd", cc_d, 32, wview("w3b"), 8, r1_d, TL, AF.Copy,
             scale_sb=s3h_sb, out_dt=F32, dve="res", res_d=x_f)
    if stage != 1:
        P_ln("l2", r1_d, xn2_d, TL, True, sum_out=lnr_d)
        P_stream("wxp", xn2_d, 8, wview("winb"), 16, xin_d, TL,
                 AF.Identity, bias_sb=binx_sb, out_dt=BF16, dve="mask",
                 store_off=PAD)
        P_stream("wzp", xn2_d, 8, wview("winb"), 16, z_d, TL,
                 AF.Identity, bias_sb=binz_sb, out_dt=F32, wcol_off=2048)
        P_conv()
        P_wxdt()
        if stage != 2:
            P_scan("a")
            P_scan("b")
            P_gate()
            P_stream("wop", yg_d, 16, wview("woutb"), 8, r2_d, TPOST,
                     AF.Identity, bias_sb=bout_sb, out_dt=F32, dve="res",
                     res_d=lnr_d, res_off=PO)
            if stage != 3:
                P_ln("l3", r2_d, xn3_d, TPOST, False)
                P_ffn_in("f2", xn3_d, a2_d, wview("w1b"), wview("w2b"),
                         TPOST)
                P_stream("g2", a2_d, 32, wview("h1b"), 32, bb2_d, TPOST,
                         AF.Gelu, scale_sb=sh1_sb)
                P_stream("g3", bb2_d, 32, wview("h2b"), 32, cc2_d, TPOST,
                         AF.Gelu, scale_sb=sh2_sb)
                P_stream("g4", cc2_d, 32, wview("w3b"), 8, out_f, TPOST,
                         AF.Copy, scale_sb=s3h_sb, out_dt=F32, dve="res",
                         res_d=r2_d)
    return nc, st


def _vec_pg(v, G):
    return np.ascontiguousarray(np.asarray(v, np.float32).reshape(G, 128).T)


def prep_inputs(I):
    x = np.asarray(I["L0_attn_x"], np.float32)
    w = {k: np.asarray(v, np.float32) for k, v in I.items()}
    com = {}
    com["wall"] = np.concatenate([
        np.ascontiguousarray(w["L0_ffn_w1_w"].T).ravel(),
        np.ascontiguousarray(w["L0_ffn_w2_w"].T).ravel(),
        np.ascontiguousarray(w["L0_ffn_h1_w"].T).ravel(),
        np.ascontiguousarray(w["L0_ffn_h2_w"].T).ravel(),
        np.ascontiguousarray(w["L0_ffn_w3_w"].T).ravel(),
        np.ascontiguousarray(w["L0_attn_W_in"].T).ravel(),
        np.ascontiguousarray(w["L0_attn_W_out"].T).ravel()])
    wxp = np.zeros((ED, 128), np.float32)
    wxp[:, 0:96] = w["L0_attn_W_x"].T
    com["wxT"] = wxp
    com["wdtT"] = np.ascontiguousarray(w["L0_attn_W_dt"].T)
    com["gammav"] = _vec_pg(w["L0_attn_gamma"], 8)
    com["betav"] = _vec_pg(w["L0_attn_beta"], 8)
    com["bdtv"] = _vec_pg(w["L0_attn_b_dt"], 16)
    com["dparv"] = _vec_pg(w["L0_attn_D_param"], 16)
    com["alogv"] = np.ascontiguousarray(
        w["L0_attn_A_log"].reshape(16, 128, NS).transpose(1, 0, 2)
        .reshape(128, 16 * NS))
    cwb = np.zeros((128, 16, KC + 1), np.float32)
    cwb[:, :, 0:KC] = w["L0_attn_conv_w"].reshape(16, 128, KC).transpose(
        1, 0, 2)
    cwb[:, :, KC] = w["L0_attn_conv_b"].reshape(16, 128).T
    com["convwbv"] = cwb
    com["eyev"] = np.eye(128, dtype=np.float32)
    com["s1v"] = _vec_pg(w["L0_ffn_w1_s"], 32)
    com["s2v"] = _vec_pg(w["L0_ffn_w2_s"], 32)
    com["sh1v"] = _vec_pg(w["L0_ffn_h1_s"], 32)
    com["sh2v"] = _vec_pg(w["L0_ffn_h2_s"], 32)
    com["s3v"] = _vec_pg(w["L0_ffn_w3_s"], 8)
    com["binxv"] = _vec_pg(w["L0_attn_b_in"][:ED], 16)
    com["binzv"] = _vec_pg(w["L0_attn_b_in"][ED:], 16)
    com["boutv"] = _vec_pg(w["L0_attn_b_out"], 8)

    in_maps = []
    for c in range(8):
        b_, half = c // 2, c % 2
        if half == 1:
            xf = np.ascontiguousarray(x[b_].T)
            mk = np.ones((1, TL), np.float32)
        else:
            xf = np.concatenate(
                [np.zeros((D, TPOST), np.float32),
                 np.ascontiguousarray(x[b_, :TPOST].T)], axis=1)
            mk = np.concatenate(
                [np.zeros((1, TPOST), np.float32),
                 np.ones((1, TL - TPOST), np.float32)], axis=1)
        m = dict(com)
        m["x_f"] = np.ascontiguousarray(xf.reshape(8, 128, TL))
        m["maskv"] = np.ascontiguousarray(mk)
        in_maps.append(m)
    return in_maps


def kernel(**inputs):
    nc, st = build(stage=0)
    in_maps = prep_inputs(inputs)
    try:
        res = run_bass_kernel_spmd(nc, in_maps, core_ids=list(range(8)))
    finally:
        st.close()
    out = np.empty((4, 8192, D), np.float32)
    for c in range(8):
        b_, half = c // 2, c % 2
        of = res.results[c]["out_f"].reshape(D, TPOST)
        out[b_, half * TPOST:(half + 1) * TPOST, :] = of.T
    return out
